# version 15
# baseline (speedup 1.0000x reference)
import numpy as np
import ml_dtypes
import concourse.bass as bass
import concourse.mybir as mybir
from concourse.bass_utils import run_bass_kernel_spmd

F32 = mybir.dt.float32
BF16 = mybir.dt.bfloat16
AF = mybir.ActivationFunctionType
ALU = mybir.AluOpType
AX = mybir.AxisListType

NCORES = 8
TPC = 4096
NT = 32
SEQT = 16
D = 1024
I = 2816
HW = I // 2
NKC = D // 128
NKI = I // 128
H, DH = 16, 64
EPS_LN = 1e-5
EPS_ATT = 1e-6

ENGS = ("pe", "act", "dve", "pool", "sp")


class Buf:
    __slots__ = ("name", "writers", "readers")

    def __init__(self, name):
        self.name = name
        self.writers = {}
        self.readers = {}


class Prog:
    ENG_SEMS = ("pe", "act", "dve", "pool")

    def __init__(self, nc):
        self.nc = nc
        self.ops = {e: [] for e in ENGS}
        self.sem_keys = ["pe", "act", "dve", "pool"]
        self.counts = {k: 0 for k in self.sem_keys}
        self.waited = {e: {} for e in ENGS}
        self.bufs = {}
        self.drains = {k: [] for k in self.ENG_SEMS}
        self.phase = "init"

    def _drain_at_or_after(self, skey, cnt):
        import os
        if not os.environ.get("KDRAIN"):
            return cnt
        import bisect
        dl = self.drains[skey]
        i = bisect.bisect_left(dl, cnt)
        if i < len(dl):
            return dl[i]
        self.counts[skey] += 1
        dc = self.counts[skey]
        self.ops[skey].append(([], lambda e: e.drain(), skey, 1, self.phase))
        dl.append(dc)
        return dc

    def stream(self, name):
        k = "dma_" + name
        if k not in self.counts:
            self.sem_keys.append(k)
            self.counts[k] = 0
        return k

    def B(self, name):
        if name not in self.bufs:
            self.bufs[name] = Buf(name)
        return self.bufs[name]

    PSUM_BUFS = frozenset(["psA", "psB", "psC", "psD", "psE", "psF", "psG", "psT"])

    def op(self, eng, emit_fn, reads=(), writes=(), append=(), dma=None, stream=None):
        deps = []
        for b in reads:
            deps += list(b.writers.items())
            if b.name in self.PSUM_BUFS:
                deps += list(b.readers.items())
        for b in writes:
            deps += list(b.writers.items())
            deps += list(b.readers.items())
        waits = []
        for (skey, cnt) in deps:
            if cnt <= 0:
                continue
            if skey in self.ENG_SEMS and skey != eng:
                cnt = self._drain_at_or_after(skey, cnt)
            if self.waited[eng].get(skey, 0) >= cnt:
                continue
            self.waited[eng][skey] = cnt
            waits.append((skey, cnt))
        skey, amt = (eng, 1) if dma is None else (self.stream(stream), 16)
        self.counts[skey] += amt
        cnt = self.counts[skey]
        self.ops[eng].append((waits, emit_fn, skey, amt, self.phase))
        for b in reads:
            b.readers[skey] = cnt
        for b in writes:
            b.writers = {skey: cnt}
            b.readers = {}
        for b in append:
            b.writers[skey] = cnt


def ts32(i):
    return slice(i * 128, (i + 1) * 128)


def build_nc():
    import os
    nc = bass.Bass("TRN2", target_bir_lowering=False)

    x0_h = nc.dram_tensor("x0", [TPC, D], F32, kind="ExternalInput").ap()
    wg_h = nc.dram_tensor("wg", [I, D], F32, kind="ExternalInput").ap()
    wu_h = nc.dram_tensor("wu", [I, D], F32, kind="ExternalInput").ap()
    wd_h = nc.dram_tensor("wd", [D, I], F32, kind="ExternalInput").ap()
    wqp_h = nc.dram_tensor("wqp", [128, H * DH], F32, kind="ExternalInput").ap()
    wkp_h = nc.dram_tensor("wkp", [128, H * DH], F32, kind="ExternalInput").ap()
    wvp_h = nc.dram_tensor("wvp", [128, H * DH], F32, kind="ExternalInput").ap()
    wo_h = nc.dram_tensor("wo", [D, D], F32, kind="ExternalInput").ap()
    id_h = nc.dram_tensor("ident", [128, 128], BF16, kind="ExternalInput").ap()
    out_h = nc.dram_tensor("xout", [TPC, D], F32, kind="ExternalOutput").ap()
    x1_h = nc.dram_tensor("x1scr", [TPC, D], F32, kind="Internal").ap()
    x2_h = nc.dram_tensor("x2scr", [TPC, D], F32, kind="Internal").ap()
    qfT_h = nc.dram_tensor("qfTscr", [NT, 128, D], BF16, kind="Internal").ap()

    wgT = nc.alloc_sbuf_tensor("wgT", [128, NKC * I], BF16)
    wuT = nc.alloc_sbuf_tensor("wuT", [128, NKC * I], BF16)
    wdT = nc.alloc_sbuf_tensor("wdT", [128, NKI * D], BF16)
    ident = nc.alloc_sbuf_tensor("idents", [128, 128], BF16)
    meanA = nc.alloc_sbuf_tensor("meanA", [128, NT], F32)
    varA = nc.alloc_sbuf_tensor("varA", [128, NT], F32)
    rstdA = nc.alloc_sbuf_tensor("rstdA", [128, NT], F32)
    st12 = nc.alloc_sbuf_tensor("st12", [128, 12], F32)
    mvtmp = nc.alloc_sbuf_tensor("mvtmp", [128, 2], F32)
    xbuf = nc.alloc_sbuf_tensor("xbuf", [128, D], F32)
    xln = nc.alloc_sbuf_tensor("xln", [128, D], F32)
    sgbuf = nc.alloc_sbuf_tensor("sgbuf", [128, D], F32)
    xbuf2 = nc.alloc_sbuf_tensor("xbuf2", [128, D], F32)
    hsl = [nc.alloc_sbuf_tensor("hsl%d" % i, [128, 512], F32) for i in range(2)]
    qsets = []
    for i in range(2):
        qsets.append({
            "w2": nc.alloc_sbuf_tensor("w2_%d" % i, [128, 512], F32),
            "g": nc.alloc_sbuf_tensor("gq_%d" % i, [128, 512], BF16),
            "m": [nc.alloc_sbuf_tensor("m%d_%d" % (j, i), [128, 512], BF16) for j in range(4)],
            "bm": nc.alloc_sbuf_tensor("bmq_%d" % i, [128, 32], F32),
            "s6": nc.alloc_sbuf_tensor("s6q_%d" % i, [128, 32], F32),
            "rs2": nc.alloc_sbuf_tensor("rs2q_%d" % i, [128, 32], F32),
        })
    xq = nc.alloc_sbuf_tensor("xq", [128, D], BF16)
    xqT = nc.alloc_sbuf_tensor("xqT", [128, D], BF16)
    arena = nc.alloc_sbuf_tensor("arena", [128, NKC * D], BF16)
    hq = arena[:, 0:I]
    hqT = arena[:, I:2 * I]
    xq2 = arena[:, 2 * I:2 * I + D]
    xqT2 = arena[:, 2 * I + D:2 * I + 2 * D]
    wos = arena[:]
    wqs = nc.alloc_sbuf_tensor("wqs", [128, H * DH], BF16)
    wks = nc.alloc_sbuf_tensor("wks", [128, H * DH], BF16)
    wvs = nc.alloc_sbuf_tensor("wvs", [128, H * DH], BF16)
    xlnb = xq
    xlnT = xqT
    kf = nc.alloc_sbuf_tensor("kfb", [128, H * DH], BF16)
    vsb = nc.alloc_sbuf_tensor("vsb", [128, H * 66], BF16)
    qfTb = nc.alloc_sbuf_tensor("qfTb", [128, D], BF16)
    kvsb = nc.alloc_sbuf_tensor("kvsb", [64, 2 * 1056], BF16)
    dens = nc.alloc_sbuf_tensor("dens", [128, H], F32)
    rden = dens
    ctx = qfTb
    ctxT = kf
    consts = nc.alloc_sbuf_tensor("consts", [128, 12], F32)
    qfTb2 = nc.alloc_sbuf_tensor("qfTb2", [64, 2048], BF16)

    ps = {}
    for nm in ("A", "B", "C", "D", "E", "F", "G"):
        ps[nm] = nc.alloc_psum_tensor("ps" + nm, [128, 512], F32)
    psT = nc.alloc_psum_tensor("psT", [128, 128], BF16)

    P = Prog(nc)
    B = P.B

    def dma(q, out_ap, in_ap, reads=(), writes=(), append=(), stream=None, transpose=False):
        assert stream is not None
        skey = P.stream(stream)
        prev = P.counts[skey]
        tok = B("_tok_" + stream)
        if transpose:
            P.op(q, lambda e, o=out_ap, i=in_ap: e.dma_start_transpose(o, i),
                 reads=list(reads) + [tok], writes=writes, append=list(append) + [tok],
                 dma=q, stream=stream)
        else:
            P.op(q, lambda e, o=out_ap, i=in_ap: e.dma_start(out=o, in_=i),
                 reads=list(reads) + [tok], writes=writes, append=list(append) + [tok],
                 dma=q, stream=stream)

    def mm(out_ap, lhsT_ap, rhs_ap, start, stop, reads, writes=(), append=(), tp=None):
        P.op("pe", lambda e, o=out_ap, l=lhsT_ap, r=rhs_ap, s1=start, s2=stop, t=tp:
             e.matmul(o, l, r, start=s1, stop=s2, tile_position=t),
             reads=reads, writes=writes, append=append)

    TSLOT = {
        "psT": psT[:],
        "psG": ps["G"][:, 0:64].bitcast(BF16),
        "psE": ps["E"][:, 0:64].bitcast(BF16),
        "psF": ps["F"][:, 0:64].bitcast(BF16),
        "psC": ps["C"][:, 0:64].bitcast(BF16),
        "psD": ps["D"][:, 0:64].bitcast(BF16),
    }

    def trans(in_ap, reads, slot="psT"):
        P.op("pe", lambda e, i=in_ap, sl=slot: e.transpose(TSLOT[sl], i, ident[:]),
             reads=list(reads) + [B("ident")], writes=[B(slot)])
        return TSLOT[slot], B(slot)

    TSLOT32 = {
        "psG": ps["G"][:, 0:128],
        "psE": ps["E"][:, 0:128],
        "psF": ps["F"][:, 0:128],
        "psC": ps["C"][:, 0:128],
        "psD": ps["D"][:, 0:128],
    }

    def trans_mm(in_ap, reads, slot):
        ap = TSLOT32[slot]
        P.op("pe", lambda e, i=in_ap, o=ap: e.matmul(o, i, ident[:], start=True, stop=True),
             reads=list(reads) + [B("ident")], writes=[B(slot)])
        return ap, B(slot)

    def act(out_ap, in_ap, func, reads, writes=(), append=(), bias=None, scale=None):
        kw = {}
        if bias is not None:
            kw["bias"] = bias
        if scale is not None:
            kw["scale"] = scale
        P.op("act", lambda e, o=out_ap, i=in_ap, f=func, k=dict(kw): e.activation(o, i, f, **k),
             reads=reads, writes=writes, append=append)

    def vo(eng, fn, reads, writes=(), append=()):
        P.op(eng, fn, reads=reads, writes=writes, append=append)

    CONSTV = {-0.0625: 0, -0.5625: 1, -1.5625: 2, -3.0625: 3, EPS_LN: 4, -0.5: 5, -6.25: 6, -12.25: 7, -25.0: 8}

    def cbias(v):
        return consts[:, CONSTV[v]:CONSTV[v] + 1]

    def init_consts():
        for v, i in CONSTV.items():
            vo("pool", lambda e, vv=v, ii=i: e.memset(consts[:, ii:ii + 1], vv),
               reads=[], append=[B("consts")])

    def quant(src_ap, src_buf, W, seti, out_ap, out_wr, out_mode="writes"):
        q = qsets[seti]
        nb = W // 16
        sfx = "_%d" % seti
        bw2, bg = B("w2" + sfx), B("gq" + sfx)
        bbm, bs6, brs2 = B("bm" + sfx), B("s6" + sfx), B("rs2" + sfx)
        bmk = [B("m%d%s" % (j, sfx)) for j in range(4)]
        m = q["m"]
        act(q["g"][:, 0:W], src_ap, AF.Sign, reads=[src_buf], writes=[bg])
        act(q["w2"][:, 0:W], src_ap, AF.Square, reads=[src_buf], writes=[bw2])
        vo("dve", lambda e, sp=src_ap, n=nb: e.tensor_reduce(
            q["bm"][:, 0:n], sp.rearrange("p (n b) -> p n b", b=16), AX.X, ALU.max,
            apply_absolute_value=True), reads=[src_buf], writes=[bbm])
        vo("dve", lambda e, n=nb: e.tensor_scalar(q["s6"][:, 0:n], q["bm"][:, 0:n],
            1.0 / 6.0, 1e-12, ALU.mult, ALU.max), reads=[bbm], writes=[bs6])
        vo("dve", lambda e, n=nb: e.tensor_tensor(q["bm"][:, 0:n], q["s6"][:, 0:n],
            q["s6"][:, 0:n], ALU.mult), reads=[bs6], writes=[bbm])
        vo("dve", lambda e, n=nb: e.reciprocal(q["rs2"][:, 0:n], q["bm"][:, 0:n]),
            reads=[bbm], writes=[brs2])
        rsb = q["rs2"][:, 0:nb].unsqueeze(2).broadcast_to([128, nb, 16])
        vo("dve", lambda e, n=nb, r=rsb: e.tensor_tensor(
            q["w2"][:, 0:n * 16].rearrange("p (n b) -> p n b", b=16),
            q["w2"][:, 0:n * 16].rearrange("p (n b) -> p n b", b=16), r, ALU.mult),
            reads=[bw2, brs2], writes=[bw2])
        act(m[0][:, 0:W], q["w2"][:, 0:W], AF.Sign, bias=cbias(-0.0625),
            reads=[bw2, B("consts")], writes=[bmk[0]])
        act(m[1][:, 0:W], q["w2"][:, 0:W], AF.Sign, bias=cbias(-0.5625),
            reads=[bw2, B("consts")], writes=[bmk[1]])
        act(m[2][:, 0:W], q["w2"][:, 0:W], AF.Sign, bias=cbias(-1.5625),
            reads=[bw2, B("consts")], writes=[bmk[2]])
        act(m[3][:, 0:W], q["w2"][:, 0:W], AF.Sign, bias=cbias(-3.0625),
            reads=[bw2, B("consts")], writes=[bmk[3]])
        qeng = "pool" if os.environ.get("KPOOL") else "dve"
        vo(qeng, lambda e, W_=W: e.tensor_tensor(m[0][:, 0:W_], m[0][:, 0:W_],
            m[1][:, 0:W_], ALU.add), reads=[bmk[0], bmk[1]], writes=[bmk[0]])
        vo(qeng, lambda e, W_=W: e.tensor_tensor(m[2][:, 0:W_], m[2][:, 0:W_],
            m[3][:, 0:W_], ALU.add), reads=[bmk[2], bmk[3]], writes=[bmk[2]])
        vo(qeng, lambda e, W_=W: e.tensor_tensor(m[0][:, 0:W_], m[0][:, 0:W_],
            m[2][:, 0:W_], ALU.add), reads=[bmk[0], bmk[2]], writes=[bmk[0]])
        act(m[1][:, 0:W], q["w2"][:, 0:W], AF.Sign, bias=cbias(-6.25),
            reads=[bw2, B("consts")], writes=[bmk[1]])
        act(m[2][:, 0:W], q["w2"][:, 0:W], AF.Sign, bias=cbias(-12.25),
            reads=[bw2, B("consts")], writes=[bmk[2]])
        act(m[3][:, 0:W], q["w2"][:, 0:W], AF.Sign, bias=cbias(-25.0),
            reads=[bw2, B("consts")], writes=[bmk[3]])
        vo("dve", lambda e, W_=W: e.tensor_tensor(m[1][:, 0:W_], m[1][:, 0:W_],
            m[2][:, 0:W_], ALU.add), reads=[bmk[1], bmk[2]], writes=[bmk[1]])
        vo("dve", lambda e, W_=W: e.scalar_tensor_tensor(m[1][:, 0:W_], m[1][:, 0:W_],
            0.5, m[3][:, 0:W_], ALU.mult, ALU.add), reads=[bmk[1], bmk[3]], writes=[bmk[1]])
        vo("dve", lambda e, W_=W: e.scalar_tensor_tensor(m[0][:, 0:W_], m[0][:, 0:W_],
            0.25, m[1][:, 0:W_], ALU.mult, ALU.add), reads=[bmk[0], bmk[1]], writes=[bmk[0]])
        vo("dve", lambda e, W_=W: e.scalar_tensor_tensor(m[0][:, 0:W_], m[0][:, 0:W_],
            3.0, q["g"][:, 0:W_], ALU.add, ALU.mult), reads=[bmk[0], bg], writes=[bmk[0]])
        s6b = q["s6"][:, 0:nb].unsqueeze(2).broadcast_to([128, nb, 16])
        kw = {"writes": [out_wr]} if out_mode == "writes" else {"append": [out_wr]}
        vo("dve", lambda e, o=out_ap, n=nb, sb=s6b: e.tensor_tensor(
            o.rearrange("p (n b) -> p n b", b=16),
            m[0][:, 0:n * 16].rearrange("p (n b) -> p n b", b=16), sb, ALU.mult),
            reads=[bmk[0], bs6], **kw)

    def ln_stats(src_h, dep_tiles):
        bx, bst, bmv = B("xbuf"), B("st12"), B("mvtmp")
        for t in range(NT):
            rd = [B(dep_tiles + "_" + str(t))] if dep_tiles else []
            dma("sp", xbuf[:], src_h[t * 128:(t + 1) * 128, :], reads=rd, writes=[bx], stream="xio")
            vo("dve", lambda e: e.bn_stats(st12[:, 0:6], xbuf[:, 0:512]),
                reads=[bx], writes=[bst])
            vo("dve", lambda e: e.bn_stats(st12[:, 6:12], xbuf[:, 512:1024]),
                reads=[bx], append=[bst])
            vo("dve", lambda e: e.bn_aggr(mvtmp[:], st12[:]), reads=[bst], writes=[bmv])
            vo("dve", lambda e, tt=t: e.tensor_copy(meanA[:, tt:tt + 1], mvtmp[:, 0:1]),
                reads=[bmv], writes=[B("meanA_" + str(t))])
            vo("dve", lambda e, tt=t: e.tensor_copy(varA[:, tt:tt + 1], mvtmp[:, 1:2]),
                reads=[bmv], writes=[B("varA_" + str(t))])
        rd = [B("varA_" + str(t)) for t in range(NT)]
        act(rstdA[:], varA[:], AF.Ln, bias=cbias(EPS_LN), reads=rd + [B("consts")], writes=[B("lnvar")])
        act(rstdA[:], rstdA[:], AF.Exp, scale=cbias(-0.5), reads=[B("lnvar"), B("consts")], writes=[B("rstdA")])

    def weight_quant(w_h, n_row_tiles, F, dstT, tag, tslots=("psT", "psG")):
        bwq = B("hq")
        width = n_row_tiles * 128
        nkc = F // 128
        chunks = [(c, min(512, F - c)) for c in range(0, F, 512)]
        for r in range(n_row_tiles):
            for ci, (off, wdt) in enumerate(chunks):
                seti = ci % 2
                hb = hsl[seti]
                bhb = B("hsl%d" % seti)
                dma("sp", hb[:, 0:wdt], w_h[r * 128:(r + 1) * 128, off:off + wdt],
                    reads=[], writes=[bhb], stream="hload%d" % seti)
                quant(hb[:, 0:wdt], bhb, wdt, seti, hq[:, off:off + wdt], bwq,
                      "writes" if ci == 0 else "append")
            for kc in range(nkc):
                tap, tb = trans(hq[:, ts32(kc)], reads=[bwq],
                                slot=("psT", "psG")[kc % 2])
                act(dstT[:, kc * width + r * 128: kc * width + r * 128 + 128], tap,
                    AF.Copy, reads=[tb], append=[B(tag)])

    def ffn_phase(src_h, src_dep, dst_h, dst_tag):
        bxln = B("xln")
        bhq, bhqT = B("hq"), B("hqT")
        bwgT, bwuT, bwdT = B("wgT"), B("wuT"), B("wdT")
        pC, pD = B("psC"), B("psD")
        slices = [(0, 512), (512, 512), (1024, 384), (1408, 512), (1920, 512), (2432, 384)]

        def xqt_of(t):
            if t % 2 == 0:
                return xq, B("xq"), xqT, B("xqT")
            return xq2, B("xq2"), xqT2, B("xqT2")

        def front(t):
            xb = xbuf if t % 2 == 0 else xbuf2
            bx = B("xbuf" if t % 2 == 0 else "xbuf2")
            xqv, bxq, xqTv, bxqT = xqt_of(t)
            rd = [B(src_dep + "_" + str(t))] if src_dep else []
            dma("sp", xb[:], src_h[t * 128:(t + 1) * 128, :], reads=rd, writes=[bx],
                stream="xio" if t % 2 == 0 else "xio2")
            vo("dve", lambda e, tt=t, x_=xb: e.tensor_scalar(xln[:], x_[:],
                meanA[:, tt:tt + 1], rstdA[:, tt:tt + 1], ALU.subtract, ALU.mult),
                reads=[bx, B("rstdA"), B("meanA_" + str(t))], writes=[bxln])
            quant(xln[:, 0:512], bxln, 512, 0, xqv[:, 0:512], bxq, "writes")
            quant(xln[:, 512:1024], bxln, 512, 1, xqv[:, 512:1024], bxq, "append")
            for kc in range(NKC):
                tap, tb = trans(xqv[:, ts32(kc)], reads=[bxq],
                                slot=("psT", "psG")[kc % 2])
                act(xqTv[:, ts32(kc)], tap, AF.Copy,
                    reads=[tb], writes=[bxqT] if kc == 0 else (),
                    append=() if kc == 0 else [bxqT])

        front(0)
        for t in range(NT):
            xb = xbuf if t % 2 == 0 else xbuf2
            bx = B("xbuf" if t % 2 == 0 else "xbuf2")
            xqv, bxq, xqTv, bxqT = xqt_of(t)
            for si, (off, wdt) in enumerate(slices):
                gbank, gname = (ps["A"], "psA") if si % 2 == 0 else (ps["E"], "psE")
                ubank, uname = (ps["B"], "psB") if si % 2 == 0 else (ps["F"], "psF")
                pG_, pU_ = B(gname), B(uname)
                for kc in range(NKC):
                    mm(gbank[:, 0:wdt], xqTv[:, ts32(kc)],
                       wgT[:, kc * I + off: kc * I + off + wdt],
                       start=(kc == 0), stop=(kc == NKC - 1),
                       reads=[bxqT, bwgT], writes=[pG_] if kc == 0 else (),
                       append=() if kc == 0 else [pG_])
                for kc in range(NKC):
                    mm(ubank[:, 0:wdt], xqTv[:, ts32(kc)],
                       wuT[:, kc * I + off: kc * I + off + wdt],
                       start=(kc == 0), stop=(kc == NKC - 1),
                       reads=[bxqT, bwuT], writes=[pU_] if kc == 0 else (),
                       append=() if kc == 0 else [pU_])
                seti = si % 2
                hb = hsl[seti]
                bhb = B("hsl%d" % seti)
                act(hb[:, 0:wdt], gbank[:, 0:wdt], AF.Silu, reads=[pG_], writes=[bhb])
                vo("dve", lambda e, b_=hb, u=ubank, w=wdt: e.tensor_tensor(
                    b_[:, 0:w], b_[:, 0:w], u[:, 0:w], ALU.mult),
                    reads=[bhb, pU_], writes=[bhb])
                quant(hb[:, 0:wdt], bhb, wdt, seti, hq[:, off:off + wdt], bhq,
                      "writes" if si == 0 else "append")
                for kc in range(off // 128, (off + wdt) // 128):
                    tap, tb = trans(hq[:, ts32(kc)], reads=[bhq],
                                    slot=("psT", "psG")[kc % 2])
                    act(hqT[:, ts32(kc)], tap, AF.Copy,
                        reads=[tb], writes=[bhqT] if kc == 0 else (),
                        append=() if kc == 0 else [bhqT])
            if t + 1 < NT:
                front(t + 1)
            for kc in range(NKI):
                mm(ps["C"][:], hqT[:, ts32(kc)], wdT[:, kc * D: kc * D + 512],
                   start=(kc == 0), stop=(kc == NKI - 1), reads=[bhqT, bwdT],
                   writes=[pC] if kc == 0 else (), append=() if kc == 0 else [pC])
            for kc in range(NKI):
                mm(ps["D"][:], hqT[:, ts32(kc)], wdT[:, kc * D + 512: kc * D + 1024],
                   start=(kc == 0), stop=(kc == NKI - 1), reads=[bhqT, bwdT],
                   writes=[pD] if kc == 0 else (), append=() if kc == 0 else [pD])
            vo("dve", lambda e, x_=xb: e.scalar_tensor_tensor(x_[:, 0:512], ps["C"][:], 0.5,
                x_[:, 0:512], ALU.mult, ALU.add), reads=[pC], writes=[bx])
            vo("dve", lambda e, x_=xb: e.scalar_tensor_tensor(x_[:, 512:1024], ps["D"][:], 0.5,
                x_[:, 512:1024], ALU.mult, ALU.add), reads=[pD], append=[bx])
            dma("act", dst_h[t * 128:(t + 1) * 128, :], xb[:], reads=[bx],
                writes=[B(dst_tag + "_" + str(t))], stream="xio" if t % 2 == 0 else "xio2")

    def attn_a():
        import os
        KA = int(os.environ.get("KATTN", "99"))
        bx, bxln = B("xbuf"), B("xln")
        bxlnb, bxlnT = B("xq"), B("xqT")
        bkf, bvs, bqf = B("kfb"), B("vsb"), B("qfTb")
        pA, pB, pE, pF = B("psA"), B("psB"), B("psE"), B("psF")
        pG, pC = B("psG"), B("psC")
        bph1, bph2 = B("xln"), B("sgbuf")
        vo("dve", lambda e: e.memset(
            vsb[:].rearrange("p (h c) -> p h c", c=66)[:, :, 64:65], 1.0),
            reads=[], writes=[B("vones")])
        for t in range(NT):
            seq = t // SEQT
            first = (t % SEQT) == 0
            last = (t % SEQT) == SEQT - 1
            dma("sp", xbuf[:], x1_h[t * 128:(t + 1) * 128, :],
                reads=[B("x1_" + str(t))], writes=[bx], stream="xio")
            if os.environ.get("KV1"):
                vo("dve", lambda e: e.tensor_copy(xln[:], xbuf[:]), reads=[bx], writes=[bxln])
            else:
                vo("dve", lambda e, tt=t: e.tensor_scalar(xln[:], xbuf[:],
                    meanA[:, tt:tt + 1], rstdA[:, tt:tt + 1], ALU.subtract, ALU.mult),
                    reads=[bx, B("rstdA"), B("meanA_" + str(t))], writes=[bxln])
            vo("dve", lambda e: e.tensor_copy(xlnb[:], xln[:]), reads=[bxln], writes=[bxlnb])
            for kc in range(NKC):
                trans(xlnb[:, ts32(kc)], reads=[bxlnb] + ([bqf, bkf, bvs] if t > 0 else []))
                vo("dve", lambda e, kk=kc: e.tensor_copy(xlnT[:, ts32(kk)], psT[:]),
                    reads=[B("psT")], writes=[bxlnT] if kc == 0 else (),
                    append=() if kc == 0 else [bxlnT])
            if KA < 2:
                continue
            for h in range(H):
                par, j = h % 2, h // 2
                bank = ps["E"] if j < 4 else ps["F"]
                pb = pE if j < 4 else pF
                st = (par == 0 and j % 4 == 0)
                mm(bank[64 * par:64 * par + 64, 128 * (j % 4):128 * (j % 4) + 128],
                   wqs[:, h * DH:(h + 1) * DH], xlnT[:, ts32(h // 2)],
                   start=True, stop=True, reads=[B("wqs"), bxlnT],
                   writes=[pb] if st else (), append=() if st else [pb],
                   tp=(0, 64 * par))
            KPHI = int(os.environ.get("KPHI", "4"))
            def phi(dst_ap, dst_b, bankL, bankR, pbL, pbR):
                vo("dve", lambda e, s=bankL: e.tensor_single_scalar(
                    xln[:, 0:512], s[:], 0.0, ALU.min), reads=[pbL], writes=[bph1])
                vo("dve", lambda e, s=bankR: e.tensor_single_scalar(
                    xln[:, 512:1024], s[:], 0.0, ALU.min), reads=[pbR], append=[bph1])
                if KPHI >= 2:
                    if os.environ.get("KV4"):
                        act(sgbuf[:], xln[:], AF.Relu, reads=[bph1], writes=[bph2])
                    elif os.environ.get("KV5"):
                        act(sgbuf[:, 0:512], bankL[:], AF.Relu, reads=[pbL, bph1], writes=[bph2])
                        act(sgbuf[:, 512:1024], bankR[:], AF.Relu, reads=[pbR], append=[bph2])
                    else:
                        act(sgbuf[:, 0:512], bankL[:], AF.Relu, reads=[pbL], writes=[bph2])
                        act(sgbuf[:, 512:1024], bankR[:], AF.Relu, reads=[pbR], append=[bph2])
                if KPHI >= 3:
                    act(xln[:], xln[:], AF.Exp, reads=[bph1], writes=[bph1])
                if KPHI >= 4:
                    vo("dve", lambda e, d=dst_ap: e.tensor_tensor(d, xln[:], sgbuf[:], ALU.add),
                        reads=[bph1, bph2], writes=[dst_b])
                else:
                    vo("dve", lambda e, d=dst_ap: e.tensor_copy(d, xln[:]),
                        reads=[bph1], writes=[dst_b])
            if KA < 3:
                continue
            phi(qfTb[:], bqf, ps["E"], ps["F"], pE, pF)
            KQF = int(os.environ.get("KQF", "2"))
            if KQF == 1:
                dma("sp", qfT_h[t], qfTb[:], reads=[bqf], writes=[B("qfT_" + str(t))], stream="qfT")
            elif KQF == 2:
                dma("act", qfT_h[t], qfTb[:], reads=[bqf], writes=[B("qfT_" + str(t))], stream="qfT")
            if KA < 4:
                continue
            for h in range(H):
                bank = ps["A"] if h < 8 else ps["B"]
                pb = pA if h < 8 else pB
                st = (h % 8 == 0)
                mm(bank[:, 64 * (h % 8):64 * (h % 8) + 64], xlnT[:, ts32(h // 2)],
                   wks[:, h * DH:(h + 1) * DH], start=True, stop=True,
                   reads=[B("wks"), bxlnT], writes=[pb] if st else (),
                   append=() if st else [pb])
            phi(kf[:], bkf, ps["A"], ps["B"], pA, pB)
            for h in range(H):
                bank = ps["A"] if h < 8 else ps["B"]
                pb = pA if h < 8 else pB
                st = (h % 8 == 0)
                mm(bank[:, 64 * (h % 8):64 * (h % 8) + 64], xlnT[:, ts32(h // 2)],
                   wvs[:, h * DH:(h + 1) * DH], start=True, stop=True,
                   reads=[B("wvs"), bxlnT], writes=[pb] if st else (),
                   append=() if st else [pb])
            for bi, (bank, pb) in enumerate(((ps["A"], pA), (ps["B"], pB))):
                vo("dve", lambda e, b=bank, off=bi * 8: e.tensor_copy(
                    vsb[:].rearrange("p (h c) -> p h c", c=66)[:, off:off + 8, 0:64],
                    b[:].rearrange("p (h c) -> p h c", c=64)),
                    reads=[pb], writes=[bvs] if bi == 0 else (),
                    append=() if bi == 0 else [bvs])
            if KA < 6:
                continue
            pD2 = B("psD")
            for h in range(H):
                if h < 7:
                    bank, pb, slot = ps["G"], pG, h
                elif h < 14:
                    bank, pb, slot = ps["C"], pC, h - 7
                else:
                    bank, pb, slot = ps["D"], pD2, h - 14
                st = (first and slot == 0)
                mm(bank[0:64, 65 * slot:65 * slot + 65],
                   kf[:, h * DH:(h + 1) * DH], vsb[:, h * 66:h * 66 + 65],
                   start=first, stop=last,
                   reads=[bkf, bvs, B("vones")],
                   writes=[pb] if st else (), append=() if st else [pb])
            if last:
                for bi, (bank, pb, nsl) in enumerate(((ps["G"], pG, 7), (ps["C"], pC, 7),
                                                      (ps["D"], pD2, 2))):
                    h0 = (0, 7, 14)[bi]
                    vo("dve", lambda e, b=bank, s=seq, hh0=h0, nn=nsl: e.tensor_copy(
                        kvsb[0:64, s * 1056 + 66 * hh0: s * 1056 + 66 * (hh0 + nn)].rearrange(
                            "p (s2 c) -> p s2 c", c=66)[:, :, 0:65],
                        b[0:64, 0:65 * nn].rearrange("p (s2 c) -> p s2 c", c=65)),
                        reads=[pb], writes=[B("kv_%d_%d" % (seq, bi))])

    def attn_b():
        import os
        KB = int(os.environ.get("KB", "9"))
        bx, bqf = B("xbuf"), B("qfTb")
        bctx, bctxT = B("qfTb"), B("kfb")
        bden, brden = B("dens"), B("rdens")
        pE, pF, pA = B("psE"), B("psF"), B("psA")
        pC, pD = B("psC"), B("psD")
        bwos = B("wos")
        for t in range(NT):
            seq = t // SEQT
            kvr = [B("kv_%d_%d" % (seq, bi)) for bi in range(3)]
            dma("sp", xbuf[:], x1_h[t * 128:(t + 1) * 128, :],
                reads=[B("x1_" + str(t))], writes=[bx], stream="xio")
            dma("act", qfTb2.ap().rearrange("d (b par c) -> d b par c", par=2, c=128),
                qfT_h[t].rearrange("(par d) (b c) -> d b par c", par=2, c=128),
                reads=[B("qfT_" + str(t))], writes=[bqf], stream="qfT")
            if KB < 2:
                dma("sp", x2_h[t * 128:(t + 1) * 128, :], xbuf[:], reads=[bx],
                    writes=[B("x2_" + str(t))], stream="xio")
                continue
            for h in range(H):
                par, j = h % 2, h // 2
                if h < 7:
                    bank, pb, slot = ps["E"], pE, h
                elif h < 14:
                    bank, pb, slot = ps["F"], pF, h - 7
                else:
                    bank, pb, slot = ps["A"], pA, h - 14
                mm(bank[:, slot * 65:slot * 65 + 65],
                   qfTb2[0:64, 128 * h:128 * h + 128],
                   kvsb[0:64, seq * 1056 + 66 * h: seq * 1056 + 66 * h + 65],
                   start=True, stop=True, reads=[bqf] + kvr,
                   writes=[pb] if slot == 0 else (), append=() if slot == 0 else [pb])
            if KB < 3:
                vo("dve", lambda e: e.tensor_copy(xbuf[:, 0:455], ps["E"][:, 0:455]),
                    reads=[pE], writes=[bx])
                vo("dve", lambda e: e.tensor_copy(xbuf[:, 455:910], ps["F"][:, 0:455]),
                    reads=[pF], append=[bx])
                vo("dve", lambda e: e.tensor_copy(xbuf[:, 910:1024], ps["A"][:, 0:114]),
                    reads=[pA], append=[bx])
                dma("sp", x2_h[t * 128:(t + 1) * 128, :], xbuf[:], reads=[bx],
                    writes=[B("x2_" + str(t))], stream="xio")
                continue
            vo("dve", lambda e: e.tensor_copy(dens[:, 0:7],
                ps["E"][:, 0:455].rearrange("p (s c) -> p s c", c=65)[:, :, 64]),
                reads=[pE], writes=[bden])
            vo("dve", lambda e: e.tensor_copy(dens[:, 7:14],
                ps["F"][:, 0:455].rearrange("p (s c) -> p s c", c=65)[:, :, 64]),
                reads=[pF], append=[bden])
            vo("dve", lambda e: e.tensor_copy(dens[:, 14:16],
                ps["A"][:, 0:130].rearrange("p (s c) -> p s c", c=65)[:, :, 64]),
                reads=[pA], append=[bden])
            vo("dve", lambda e: e.tensor_single_scalar(dens[:], dens[:], EPS_ATT, ALU.add),
                reads=[bden], writes=[bden])
            vo("dve", lambda e: e.reciprocal(rden[:], dens[:]), reads=[bden], writes=[brden, bden])
            if KB < 4:
                dma("sp", x2_h[t * 128:(t + 1) * 128, :], xbuf[:], reads=[bx],
                    writes=[B("x2_" + str(t))], stream="xio")
                continue
            for (bank, pb, h0, nh) in ((ps["E"], pE, 0, 7), (ps["F"], pF, 7, 7),
                                       (ps["A"], pA, 14, 2)):
                rb = rden[:, h0:h0 + nh].unsqueeze(2).broadcast_to([128, nh, 64])
                vo("dve", lambda e, b=bank, hh=h0, nn=nh, r=rb: e.tensor_tensor(
                    ctx[:, hh * 64:(hh + nn) * 64].rearrange("p (s c) -> p s c", c=64),
                    b[:, 0:nn * 65].rearrange("p (s c) -> p s c", c=65)[:, :, 0:64],
                    r, ALU.mult),
                    reads=[pb, brden], writes=[bctx] if h0 == 0 else (),
                    append=() if h0 == 0 else [bctx])
            if KB < 5:
                dma("sp", x2_h[t * 128:(t + 1) * 128, :], xbuf[:], reads=[bx],
                    writes=[B("x2_" + str(t))], stream="xio")
                continue
            for kc in range(NKC):
                tap, tb = trans(ctx[:, ts32(kc)], reads=[bctx],
                                slot=("psT", "psG")[kc % 2])
                act(ctxT[:, ts32(kc)], tap, AF.Copy,
                    reads=[tb], writes=[bctxT] if kc == 0 else (),
                    append=() if kc == 0 else [bctxT])
            for kc in range(NKC):
                mm(ps["C"][:], ctxT[:, ts32(kc)], wos[:, kc * D: kc * D + 512],
                   start=(kc == 0), stop=(kc == NKC - 1), reads=[bctxT, bwos],
                   writes=[pC] if kc == 0 else (), append=() if kc == 0 else [pC])
            for kc in range(NKC):
                mm(ps["D"][:], ctxT[:, ts32(kc)], wos[:, kc * D + 512: kc * D + 1024],
                   start=(kc == 0), stop=(kc == NKC - 1), reads=[bctxT, bwos],
                   writes=[pD] if kc == 0 else (), append=() if kc == 0 else [pD])
            vo("dve", lambda e: e.tensor_tensor(xbuf[:, 0:512], xbuf[:, 0:512],
                ps["C"][:], ALU.add), reads=[pC], writes=[bx], append=())
            vo("dve", lambda e: e.tensor_tensor(xbuf[:, 512:1024], xbuf[:, 512:1024],
                ps["D"][:], ALU.add), reads=[pD], append=[bx])
            dma("sp", x2_h[t * 128:(t + 1) * 128, :], xbuf[:], reads=[bx],
                writes=[B("x2_" + str(t))], stream="xio")

    import os
    NPH = int(os.environ.get("KPHASES", "99"))
    init_consts()
    dma("sp", ident[:], id_h, reads=[], writes=[B("ident")], stream="ident")
    if os.environ.get("KV2"):
        vo("pool", lambda e: e.memset(wqs[:], 0.01), reads=[], writes=[B("wqs")])
        vo("pool", lambda e: e.memset(wks[:], 0.01), reads=[], writes=[B("wks")])
        vo("pool", lambda e: e.memset(wvs[:], 0.01), reads=[], writes=[B("wvs")])
    else:
        dma("pool", wqs[:], wqp_h, reads=[], writes=[B("wqs")], stream="wq")
        dma("pool", wks[:], wkp_h, reads=[], writes=[B("wks")], stream="wk")
        dma("pool", wvs[:], wvp_h, reads=[], writes=[B("wvs")], stream="wv")

    def _setphase(nm):
        P.phase = nm

    phases = [
        lambda: (_setphase("p1pre"), ln_stats(x0_h, None)),
        lambda: (_setphase("p0"), weight_quant(wg_h, NKI, D, wgT, "wgT")),
        lambda: weight_quant(wu_h, NKI, D, wuT, "wuT"),
        lambda: weight_quant(wd_h, NKC, I, wdT, "wdT"),
        lambda: (_setphase("p1"), ffn_phase(x0_h, None, x1_h, "x1")),
        lambda: (_setphase("p2pre"), ln_stats(x1_h, "x1")),
        lambda: (_setphase("p2a"), attn_a()),
        lambda: (vo("pool", lambda e: e.nop(), reads=[],
                    writes=[B("hq"), B("hqT"), B("xq2"), B("xqT2"), B("wos")]),
                 dma("pool", wos.rearrange("p (c o) -> p c o", o=D),
                     wo_h.rearrange("(c p) o -> p c o", p=128),
                     reads=[], writes=[B("wos")], stream="wos"),
                 _setphase("p2b"), attn_b()),
        lambda: (_setphase("p3pre"), ln_stats(x2_h, "x2")),
        lambda: (vo("pool", lambda e: e.nop(), reads=[],
                    writes=[B("hq"), B("hqT"), B("xq2"), B("xqT2"), B("wos")]),
                 _setphase("p3"), ffn_phase(x2_h, "x2", out_h, "out")),
    ]
    PSET = os.environ.get("KPSET", "")
    if PSET:
        sel = set(int(x) for x in PSET.split(","))
        for i, ph in enumerate(phases):
            if i in sel:
                ph()
    else:
        for i, ph in enumerate(phases):
            if i < NPH:
                ph()

    final_counts = dict(P.counts)

    sems = {k: nc.alloc_semaphore("sem_" + k) for k in P.sem_keys}
    eng_map = {"pe": "tensor", "act": "scalar", "dve": "vector",
               "pool": "gpsimd", "sp": "sync"}

    with nc.Block() as block:
        def emit_for(eng_name):
            def body(eng):
                cur = None
                import contextlib
                ctxm = [contextlib.nullcontext()]
                for (waits, fn, skey, amt, phn) in P.ops[eng_name]:
                    if phn != cur:
                        ctxm[0].__exit__(None, None, None) if cur is not None else None
                        ctxm[0] = nc.named_scope(phn)
                        ctxm[0].__enter__()
                        cur = phn
                    for (wk, wv) in waits:
                        eng.wait_ge(sems[wk], wv)
                    fn(eng).then_inc(sems[skey], amt)
                if cur is not None:
                    ctxm[0].__exit__(None, None, None)
                if eng_name == "sp":
                    for k in P.sem_keys:
                        if final_counts[k] > 0:
                            eng.wait_ge(sems[k], final_counts[k])
            return body

        for eng_name, attr in eng_map.items():
            getattr(block, attr)(emit_for(eng_name))

    return nc


_CACHE = {}


def kernel(L0_attn_query, L0_attn_Wq, L0_attn_Wk, L0_attn_Wv, L0_attn_Wo,
           L0_ffn_w_gate, L0_ffn_w_up, L0_ffn_w_down):
    x = np.ascontiguousarray(np.asarray(L0_attn_query, np.float32)).reshape(16, 2048, 1024)
    shards = [np.ascontiguousarray(x[2 * c:2 * c + 2].reshape(TPC, D)) for c in range(NCORES)]

    def pad_heads(w):
        w = np.asarray(w, np.float32)
        p = np.zeros((128, H, DH), np.float32)
        for h in range(H):
            p[64 * (h % 2):64 * (h % 2) + 64, h, :] = w[h]
        return np.ascontiguousarray(p.reshape(128, H * DH))

    common = {
        "wg": np.ascontiguousarray(np.asarray(L0_ffn_w_gate, np.float32)),
        "wu": np.ascontiguousarray(np.asarray(L0_ffn_w_up, np.float32)),
        "wd": np.ascontiguousarray(np.asarray(L0_ffn_w_down, np.float32)),
        "wqp": pad_heads(L0_attn_Wq),
        "wkp": pad_heads(L0_attn_Wk),
        "wvp": pad_heads(L0_attn_Wv),
        "wo": np.ascontiguousarray(np.asarray(L0_attn_Wo, np.float32)),
        "ident": np.eye(128, dtype=ml_dtypes.bfloat16),
    }
    if "nc" not in _CACHE:
        _CACHE["nc"] = build_nc()
    nc = _CACHE["nc"]
    in_maps = [{"x0": shards[c], **common} for c in range(NCORES)]
    res = run_bass_kernel_spmd(nc, in_maps, core_ids=list(range(NCORES)))
    out = np.empty((16, 2048, 1024), np.float32)
    for c in range(NCORES):
        out[2 * c:2 * c + 2] = np.asarray(res.results[c]["xout"], np.float32).reshape(2, 2048, 1024)
    return out


# version 16
# speedup vs baseline: 1.1747x; 1.1747x over previous
import numpy as np
import ml_dtypes
import concourse.bass as bass
import concourse.mybir as mybir
from concourse.bass_utils import run_bass_kernel_spmd

F32 = mybir.dt.float32
BF16 = mybir.dt.bfloat16
AF = mybir.ActivationFunctionType
ALU = mybir.AluOpType
AX = mybir.AxisListType

NCORES = 8
TPC = 4096
NT = 32
SEQT = 16
D = 1024
I = 2816
HW = I // 2
NKC = D // 128
NKI = I // 128
H, DH = 16, 64
EPS_LN = 1e-5
EPS_ATT = 1e-6

ENGS = ("pe", "act", "dve", "pool", "sp")


class Buf:
    __slots__ = ("name", "writers", "readers")

    def __init__(self, name):
        self.name = name
        self.writers = {}
        self.readers = {}


class Prog:
    ENG_SEMS = ("pe", "act", "dve", "pool")

    def __init__(self, nc):
        self.nc = nc
        self.ops = {e: [] for e in ENGS}
        self.sem_keys = ["pe", "act", "dve", "pool"]
        self.counts = {k: 0 for k in self.sem_keys}
        self.waited = {e: {} for e in ENGS}
        self.bufs = {}
        self.drains = {k: [] for k in self.ENG_SEMS}
        self.phase = "init"

    def _drain_at_or_after(self, skey, cnt):
        import os
        if not os.environ.get("KDRAIN"):
            return cnt
        import bisect
        dl = self.drains[skey]
        i = bisect.bisect_left(dl, cnt)
        if i < len(dl):
            return dl[i]
        self.counts[skey] += 1
        dc = self.counts[skey]
        self.ops[skey].append(([], lambda e: e.drain(), skey, 1, self.phase))
        dl.append(dc)
        return dc

    def stream(self, name):
        k = "dma_" + name
        if k not in self.counts:
            self.sem_keys.append(k)
            self.counts[k] = 0
        return k

    def B(self, name):
        if name not in self.bufs:
            self.bufs[name] = Buf(name)
        return self.bufs[name]

    PSUM_BUFS = frozenset(["psA", "psB", "psC", "psD", "psE", "psF", "psG", "psT"])

    def op(self, eng, emit_fn, reads=(), writes=(), append=(), dma=None, stream=None):
        deps = []
        for b in reads:
            deps += list(b.writers.items())
            if b.name in self.PSUM_BUFS:
                deps += list(b.readers.items())
        for b in writes:
            deps += list(b.writers.items())
            deps += list(b.readers.items())
        waits = []
        for (skey, cnt) in deps:
            if cnt <= 0:
                continue
            if skey in self.ENG_SEMS and skey != eng:
                cnt = self._drain_at_or_after(skey, cnt)
            if self.waited[eng].get(skey, 0) >= cnt:
                continue
            self.waited[eng][skey] = cnt
            waits.append((skey, cnt))
        skey, amt = (eng, 1) if dma is None else (self.stream(stream), 16)
        self.counts[skey] += amt
        cnt = self.counts[skey]
        self.ops[eng].append((waits, emit_fn, skey, amt, self.phase))
        for b in reads:
            b.readers[skey] = cnt
        for b in writes:
            b.writers = {skey: cnt}
            b.readers = {}
        for b in append:
            b.writers[skey] = cnt


def ts32(i):
    return slice(i * 128, (i + 1) * 128)


def build_nc():
    import os
    nc = bass.Bass("TRN2", target_bir_lowering=False)

    x0_h = nc.dram_tensor("x0", [TPC, D], F32, kind="ExternalInput").ap()
    wg_h = nc.dram_tensor("wg", [I, D], F32, kind="ExternalInput").ap()
    wu_h = nc.dram_tensor("wu", [I, D], F32, kind="ExternalInput").ap()
    wd_h = nc.dram_tensor("wd", [D, I], F32, kind="ExternalInput").ap()
    wqp_h = nc.dram_tensor("wqp", [128, H * DH], F32, kind="ExternalInput").ap()
    wkp_h = nc.dram_tensor("wkp", [128, H * DH], F32, kind="ExternalInput").ap()
    wvp_h = nc.dram_tensor("wvp", [128, H * DH], F32, kind="ExternalInput").ap()
    wo_h = nc.dram_tensor("wo", [D, D], F32, kind="ExternalInput").ap()
    id_h = nc.dram_tensor("ident", [128, 128], BF16, kind="ExternalInput").ap()
    out_h = nc.dram_tensor("xout", [TPC, D], F32, kind="ExternalOutput").ap()
    x1_h = nc.dram_tensor("x1scr", [TPC, D], F32, kind="Internal").ap()
    x2_h = nc.dram_tensor("x2scr", [TPC, D], F32, kind="Internal").ap()
    qfT_h = nc.dram_tensor("qfTscr", [NT, 128, D], BF16, kind="Internal").ap()

    wgT = nc.alloc_sbuf_tensor("wgT", [128, NKC * I], BF16)
    wuT = nc.alloc_sbuf_tensor("wuT", [128, NKC * I], BF16)
    wdT = nc.alloc_sbuf_tensor("wdT", [128, NKI * D], BF16)
    ident = nc.alloc_sbuf_tensor("idents", [128, 128], BF16)
    meanA = nc.alloc_sbuf_tensor("meanA", [128, NT], F32)
    varA = nc.alloc_sbuf_tensor("varA", [128, NT], F32)
    rstdA = nc.alloc_sbuf_tensor("rstdA", [128, NT], F32)
    st12 = nc.alloc_sbuf_tensor("st12", [128, 12], F32)
    mvtmp = nc.alloc_sbuf_tensor("mvtmp", [128, 2], F32)
    xbuf = nc.alloc_sbuf_tensor("xbuf", [128, D], F32)
    xln = nc.alloc_sbuf_tensor("xln", [128, D], F32)
    sgbuf = nc.alloc_sbuf_tensor("sgbuf", [128, D], F32)
    xbuf2 = nc.alloc_sbuf_tensor("xbuf2", [128, D], F32)
    hsl = [nc.alloc_sbuf_tensor("hsl%d" % i, [128, 512], F32) for i in range(2)]
    qsets = []
    for i in range(2):
        qsets.append({
            "w2": nc.alloc_sbuf_tensor("w2_%d" % i, [128, 512], F32),
            "g": nc.alloc_sbuf_tensor("gq_%d" % i, [128, 512], BF16),
            "m": [nc.alloc_sbuf_tensor("m%d_%d" % (j, i), [128, 512], BF16) for j in range(4)],
            "bm": nc.alloc_sbuf_tensor("bmq_%d" % i, [128, 32], F32),
            "s6": nc.alloc_sbuf_tensor("s6q_%d" % i, [128, 32], F32),
            "rs2": nc.alloc_sbuf_tensor("rs2q_%d" % i, [128, 32], F32),
        })
    xq = nc.alloc_sbuf_tensor("xq", [128, D], BF16)
    xqT = nc.alloc_sbuf_tensor("xqT", [128, D], BF16)
    arena = nc.alloc_sbuf_tensor("arena", [128, NKC * D], BF16)
    hq = arena[:, 0:I]
    hqT = arena[:, I:2 * I]
    xq2 = arena[:, 2 * I:2 * I + D]
    xqT2 = arena[:, 2 * I + D:2 * I + 2 * D]
    wos = arena[:]
    wqs = nc.alloc_sbuf_tensor("wqs", [128, H * DH], BF16)
    wks = nc.alloc_sbuf_tensor("wks", [128, H * DH], BF16)
    wvs = nc.alloc_sbuf_tensor("wvs", [128, H * DH], BF16)
    xlnb = xq
    xlnT = xqT
    kf = nc.alloc_sbuf_tensor("kfb", [128, H * DH], BF16)
    vsb = nc.alloc_sbuf_tensor("vsb", [128, H * 66], BF16)
    qfTb = nc.alloc_sbuf_tensor("qfTb", [128, D], BF16)
    kvsb = nc.alloc_sbuf_tensor("kvsb", [64, 2 * 1056], BF16)
    dens = nc.alloc_sbuf_tensor("dens", [128, H], F32)
    rden = dens
    ctx = qfTb
    ctxT = kf
    consts = nc.alloc_sbuf_tensor("consts", [128, 12], F32)
    qfTb2 = nc.alloc_sbuf_tensor("qfTb2", [64, 2048], BF16)

    ps = {}
    for nm in ("A", "B", "C", "D", "E", "F", "G"):
        ps[nm] = nc.alloc_psum_tensor("ps" + nm, [128, 512], F32)
    psT = nc.alloc_psum_tensor("psT", [128, 128], BF16)

    P = Prog(nc)
    B = P.B

    def dma(q, out_ap, in_ap, reads=(), writes=(), append=(), stream=None):
        assert stream is not None
        P.op(q, lambda e, o=out_ap, i=in_ap: e.dma_start(out=o, in_=i),
             reads=reads, writes=writes, append=append, dma=q, stream=stream)

    def mm(out_ap, lhsT_ap, rhs_ap, start, stop, reads, writes=(), append=(), tp=None):
        P.op("pe", lambda e, o=out_ap, l=lhsT_ap, r=rhs_ap, s1=start, s2=stop, t=tp:
             e.matmul(o, l, r, start=s1, stop=s2, tile_position=t),
             reads=reads, writes=writes, append=append)

    TSLOT = {
        "psT": psT[:],
        "psG": ps["G"][:, 0:64].bitcast(BF16),
        "psE": ps["E"][:, 0:64].bitcast(BF16),
        "psF": ps["F"][:, 0:64].bitcast(BF16),
        "psC": ps["C"][:, 0:64].bitcast(BF16),
        "psD": ps["D"][:, 0:64].bitcast(BF16),
    }

    def trans(in_ap, reads, slot="psT"):
        P.op("pe", lambda e, i=in_ap, sl=slot: e.transpose(TSLOT[sl], i, ident[:]),
             reads=list(reads) + [B("ident")], writes=[B(slot)])
        return TSLOT[slot], B(slot)

    def act(out_ap, in_ap, func, reads, writes=(), append=(), bias=None, scale=None):
        kw = {}
        if bias is not None:
            kw["bias"] = bias
        if scale is not None:
            kw["scale"] = scale
        P.op("act", lambda e, o=out_ap, i=in_ap, f=func, k=dict(kw): e.activation(o, i, f, **k),
             reads=reads, writes=writes, append=append)

    def vo(eng, fn, reads, writes=(), append=()):
        P.op(eng, fn, reads=reads, writes=writes, append=append)

    CONSTV = {-0.0625: 0, -0.5625: 1, -1.5625: 2, -3.0625: 3, EPS_LN: 4, -0.5: 5, -6.25: 6, -12.25: 7, -25.0: 8}

    def cbias(v):
        return consts[:, CONSTV[v]:CONSTV[v] + 1]

    def init_consts():
        for v, i in CONSTV.items():
            vo("pool", lambda e, vv=v, ii=i: e.memset(consts[:, ii:ii + 1], vv),
               reads=[], append=[B("consts")])

    def quant(src_ap, src_buf, W, seti, out_ap, out_wr, out_mode="writes"):
        q = qsets[seti]
        nb = W // 16
        sfx = "_%d" % seti
        bw2, bg = B("w2" + sfx), B("gq" + sfx)
        bbm, bs6, brs2 = B("bm" + sfx), B("s6" + sfx), B("rs2" + sfx)
        bmk = [B("m%d%s" % (j, sfx)) for j in range(4)]
        m = q["m"]
        act(q["g"][:, 0:W], src_ap, AF.Sign, reads=[src_buf], writes=[bg])
        act(q["w2"][:, 0:W], src_ap, AF.Square, reads=[src_buf], writes=[bw2])
        vo("dve", lambda e, sp=src_ap, n=nb: e.tensor_reduce(
            q["bm"][:, 0:n], sp.rearrange("p (n b) -> p n b", b=16), AX.X, ALU.max,
            apply_absolute_value=True), reads=[src_buf], writes=[bbm])
        vo("dve", lambda e, n=nb: e.tensor_scalar(q["s6"][:, 0:n], q["bm"][:, 0:n],
            1.0 / 6.0, 1e-12, ALU.mult, ALU.max), reads=[bbm], writes=[bs6])
        vo("dve", lambda e, n=nb: e.tensor_tensor(q["bm"][:, 0:n], q["s6"][:, 0:n],
            q["s6"][:, 0:n], ALU.mult), reads=[bs6], writes=[bbm])
        vo("dve", lambda e, n=nb: e.reciprocal(q["rs2"][:, 0:n], q["bm"][:, 0:n]),
            reads=[bbm], writes=[brs2])
        rsb = q["rs2"][:, 0:nb].unsqueeze(2).broadcast_to([128, nb, 16])
        vo("dve", lambda e, n=nb, r=rsb: e.tensor_tensor(
            q["w2"][:, 0:n * 16].rearrange("p (n b) -> p n b", b=16),
            q["w2"][:, 0:n * 16].rearrange("p (n b) -> p n b", b=16), r, ALU.mult),
            reads=[bw2, brs2], writes=[bw2])
        act(m[0][:, 0:W], q["w2"][:, 0:W], AF.Sign, bias=cbias(-0.0625),
            reads=[bw2, B("consts")], writes=[bmk[0]])
        act(m[1][:, 0:W], q["w2"][:, 0:W], AF.Sign, bias=cbias(-0.5625),
            reads=[bw2, B("consts")], writes=[bmk[1]])
        act(m[2][:, 0:W], q["w2"][:, 0:W], AF.Sign, bias=cbias(-1.5625),
            reads=[bw2, B("consts")], writes=[bmk[2]])
        act(m[3][:, 0:W], q["w2"][:, 0:W], AF.Sign, bias=cbias(-3.0625),
            reads=[bw2, B("consts")], writes=[bmk[3]])
        qeng = "pool" if os.environ.get("KPOOL") else "dve"
        vo(qeng, lambda e, W_=W: e.tensor_tensor(m[0][:, 0:W_], m[0][:, 0:W_],
            m[1][:, 0:W_], ALU.add), reads=[bmk[0], bmk[1]], writes=[bmk[0]])
        vo(qeng, lambda e, W_=W: e.tensor_tensor(m[2][:, 0:W_], m[2][:, 0:W_],
            m[3][:, 0:W_], ALU.add), reads=[bmk[2], bmk[3]], writes=[bmk[2]])
        vo(qeng, lambda e, W_=W: e.tensor_tensor(m[0][:, 0:W_], m[0][:, 0:W_],
            m[2][:, 0:W_], ALU.add), reads=[bmk[0], bmk[2]], writes=[bmk[0]])
        act(m[1][:, 0:W], q["w2"][:, 0:W], AF.Sign, bias=cbias(-6.25),
            reads=[bw2, B("consts")], writes=[bmk[1]])
        act(m[2][:, 0:W], q["w2"][:, 0:W], AF.Sign, bias=cbias(-12.25),
            reads=[bw2, B("consts")], writes=[bmk[2]])
        act(m[3][:, 0:W], q["w2"][:, 0:W], AF.Sign, bias=cbias(-25.0),
            reads=[bw2, B("consts")], writes=[bmk[3]])
        vo("dve", lambda e, W_=W: e.tensor_tensor(m[1][:, 0:W_], m[1][:, 0:W_],
            m[2][:, 0:W_], ALU.add), reads=[bmk[1], bmk[2]], writes=[bmk[1]])
        vo("dve", lambda e, W_=W: e.scalar_tensor_tensor(m[1][:, 0:W_], m[1][:, 0:W_],
            0.5, m[3][:, 0:W_], ALU.mult, ALU.add), reads=[bmk[1], bmk[3]], writes=[bmk[1]])
        vo("dve", lambda e, W_=W: e.scalar_tensor_tensor(m[0][:, 0:W_], m[0][:, 0:W_],
            0.25, m[1][:, 0:W_], ALU.mult, ALU.add), reads=[bmk[0], bmk[1]], writes=[bmk[0]])
        vo("dve", lambda e, W_=W: e.scalar_tensor_tensor(m[0][:, 0:W_], m[0][:, 0:W_],
            3.0, q["g"][:, 0:W_], ALU.add, ALU.mult), reads=[bmk[0], bg], writes=[bmk[0]])
        s6b = q["s6"][:, 0:nb].unsqueeze(2).broadcast_to([128, nb, 16])
        kw = {"writes": [out_wr]} if out_mode == "writes" else {"append": [out_wr]}
        vo("dve", lambda e, o=out_ap, n=nb, sb=s6b: e.tensor_tensor(
            o.rearrange("p (n b) -> p n b", b=16),
            m[0][:, 0:n * 16].rearrange("p (n b) -> p n b", b=16), sb, ALU.mult),
            reads=[bmk[0], bs6], **kw)

    def quant_dve(src_ap, src_buf, W, seti, out_ap, out_wr, out_mode="writes"):
        q = qsets[seti]
        nb = W // 16
        sfx = "_%d" % seti
        bw2, bg = B("w2" + sfx), B("gq" + sfx)
        bbm, bs6, brs2 = B("bm" + sfx), B("s6" + sfx), B("rs2" + sfx)
        bmk = [B("m%d%s" % (j, sfx)) for j in range(4)]
        m = q["m"]
        act(q["g"][:, 0:W], src_ap, AF.Sign, reads=[src_buf], writes=[bg])
        vo("dve", lambda e, sp=src_ap, W_=W: e.tensor_tensor(q["w2"][:, 0:W_], sp, sp, ALU.mult),
            reads=[src_buf], writes=[bw2])
        vo("dve", lambda e, sp=src_ap, n=nb: e.tensor_reduce(
            q["bm"][:, 0:n], sp.rearrange("p (n b) -> p n b", b=16), AX.X, ALU.max,
            apply_absolute_value=True), reads=[src_buf], writes=[bbm])
        vo("dve", lambda e, n=nb: e.tensor_scalar(q["s6"][:, 0:n], q["bm"][:, 0:n],
            1.0 / 6.0, 1e-12, ALU.mult, ALU.max), reads=[bbm], writes=[bs6])
        vo("dve", lambda e, n=nb: e.tensor_tensor(q["bm"][:, 0:n], q["s6"][:, 0:n],
            q["s6"][:, 0:n], ALU.mult), reads=[bs6], writes=[bbm])
        vo("dve", lambda e, n=nb: e.reciprocal(q["rs2"][:, 0:n], q["bm"][:, 0:n]),
            reads=[bbm], writes=[brs2])
        rsb = q["rs2"][:, 0:nb].unsqueeze(2).broadcast_to([128, nb, 16])
        vo("dve", lambda e, n=nb, r=rsb: e.tensor_tensor(
            q["w2"][:, 0:n * 16].rearrange("p (n b) -> p n b", b=16),
            q["w2"][:, 0:n * 16].rearrange("p (n b) -> p n b", b=16), r, ALU.mult),
            reads=[bw2, brs2], writes=[bw2])
        vo("dve", lambda e, W_=W: e.tensor_scalar(m[0][:, 0:W_], q["w2"][:, 0:W_],
            0.0625, 0.5, ALU.is_gt, ALU.mult), reads=[bw2], writes=[bmk[0]])
        vo("dve", lambda e, W_=W: e.tensor_scalar(m[1][:, 0:W_], q["w2"][:, 0:W_],
            0.5625, 0.5, ALU.is_gt, ALU.mult), reads=[bw2], writes=[bmk[1]])
        vo("dve", lambda e, W_=W: e.tensor_tensor(m[0][:, 0:W_], m[0][:, 0:W_],
            m[1][:, 0:W_], ALU.add), reads=[bmk[0], bmk[1]], writes=[bmk[0]])
        vo("dve", lambda e, W_=W: e.tensor_scalar(m[2][:, 0:W_], q["w2"][:, 0:W_],
            1.5625, 0.5, ALU.is_gt, ALU.mult), reads=[bw2], writes=[bmk[2]])
        vo("dve", lambda e, W_=W: e.tensor_scalar(m[3][:, 0:W_], q["w2"][:, 0:W_],
            3.0625, 0.5, ALU.is_gt, ALU.mult), reads=[bw2], writes=[bmk[3]])
        vo("dve", lambda e, W_=W: e.tensor_tensor(m[2][:, 0:W_], m[2][:, 0:W_],
            m[3][:, 0:W_], ALU.add), reads=[bmk[2], bmk[3]], writes=[bmk[2]])
        vo("dve", lambda e, W_=W: e.tensor_tensor(m[0][:, 0:W_], m[0][:, 0:W_],
            m[2][:, 0:W_], ALU.add), reads=[bmk[0], bmk[2]], writes=[bmk[0]])
        vo("dve", lambda e, W_=W: e.tensor_single_scalar(m[1][:, 0:W_], q["w2"][:, 0:W_],
            6.25, ALU.is_gt), reads=[bw2], writes=[bmk[1]])
        vo("dve", lambda e, W_=W: e.tensor_scalar(m[3][:, 0:W_], q["w2"][:, 0:W_],
            25.0, 2.0, ALU.is_gt, ALU.mult), reads=[bw2], writes=[bmk[3]])
        vo("dve", lambda e, W_=W: e.tensor_tensor(m[1][:, 0:W_], m[1][:, 0:W_],
            m[3][:, 0:W_], ALU.add), reads=[bmk[1], bmk[3]], writes=[bmk[1]])
        vo("dve", lambda e, W_=W: e.tensor_single_scalar(m[3][:, 0:W_], q["w2"][:, 0:W_],
            12.25, ALU.is_gt), reads=[bw2], writes=[bmk[3]])
        vo("dve", lambda e, W_=W: e.tensor_tensor(m[1][:, 0:W_], m[1][:, 0:W_],
            m[3][:, 0:W_], ALU.add), reads=[bmk[1], bmk[3]], writes=[bmk[1]])
        vo("dve", lambda e, W_=W: e.tensor_tensor(m[0][:, 0:W_], m[0][:, 0:W_],
            m[1][:, 0:W_], ALU.add), reads=[bmk[0], bmk[1]], writes=[bmk[0]])
        vo("dve", lambda e, W_=W: e.tensor_tensor(m[0][:, 0:W_], m[0][:, 0:W_],
            q["g"][:, 0:W_], ALU.mult), reads=[bmk[0], bg], writes=[bmk[0]])
        s6b = q["s6"][:, 0:nb].unsqueeze(2).broadcast_to([128, nb, 16])
        kw = {"writes": [out_wr]} if out_mode == "writes" else {"append": [out_wr]}
        vo("dve", lambda e, o=out_ap, n=nb, sb=s6b: e.tensor_tensor(
            o.rearrange("p (n b) -> p n b", b=16),
            m[0][:, 0:n * 16].rearrange("p (n b) -> p n b", b=16), sb, ALU.mult),
            reads=[bmk[0], bs6], **kw)

    _quant_orig = quant

    def quant_mix(src_ap, src_buf, W, seti, out_ap, out_wr, out_mode="writes"):
        fn = quant_dve if seti == 0 else _quant_orig
        fn(src_ap, src_buf, W, seti, out_ap, out_wr, out_mode)

    quant = quant_mix

    def ln_stats(src_h, dep_tiles):
        bx, bst, bmv = B("xbuf"), B("st12"), B("mvtmp")
        for t in range(NT):
            rd = [B(dep_tiles + "_" + str(t))] if dep_tiles else []
            dma("sp", xbuf[:], src_h[t * 128:(t + 1) * 128, :], reads=rd, writes=[bx], stream="xio")
            vo("dve", lambda e: e.bn_stats(st12[:, 0:6], xbuf[:, 0:512]),
                reads=[bx], writes=[bst])
            vo("dve", lambda e: e.bn_stats(st12[:, 6:12], xbuf[:, 512:1024]),
                reads=[bx], append=[bst])
            vo("dve", lambda e: e.bn_aggr(mvtmp[:], st12[:]), reads=[bst], writes=[bmv])
            vo("dve", lambda e, tt=t: e.tensor_copy(meanA[:, tt:tt + 1], mvtmp[:, 0:1]),
                reads=[bmv], writes=[B("meanA_" + str(t))])
            vo("dve", lambda e, tt=t: e.tensor_copy(varA[:, tt:tt + 1], mvtmp[:, 1:2]),
                reads=[bmv], writes=[B("varA_" + str(t))])
        rd = [B("varA_" + str(t)) for t in range(NT)]
        act(rstdA[:], varA[:], AF.Ln, bias=cbias(EPS_LN), reads=rd + [B("consts")], writes=[B("lnvar")])
        act(rstdA[:], rstdA[:], AF.Exp, scale=cbias(-0.5), reads=[B("lnvar"), B("consts")], writes=[B("rstdA")])

    def weight_quant(w_h, n_row_tiles, F, dstT, tag, tslots=("psT", "psG")):
        bwq = B("hq")
        width = n_row_tiles * 128
        nkc = F // 128
        chunks = [(c, min(512, F - c)) for c in range(0, F, 512)]
        for r in range(n_row_tiles):
            for ci, (off, wdt) in enumerate(chunks):
                seti = ci % 2
                hb = hsl[seti]
                bhb = B("hsl%d" % seti)
                dma("sp", hb[:, 0:wdt], w_h[r * 128:(r + 1) * 128, off:off + wdt],
                    reads=[], writes=[bhb], stream="hload%d" % seti)
                quant(hb[:, 0:wdt], bhb, wdt, seti, hq[:, off:off + wdt], bwq,
                      "writes" if ci == 0 else "append")
            for kc in range(nkc):
                tap, tb = trans(hq[:, ts32(kc)], reads=[bwq], slot=tslots[kc % len(tslots)])
                vo("dve", lambda e, kk=kc, rr=r, dd=dstT, w=width, t=tap: e.tensor_copy(
                    dd[:, kk * w + rr * 128: kk * w + rr * 128 + 128], t),
                    reads=[tb], append=[B(tag)])

    def ffn_phase(src_h, src_dep, dst_h, dst_tag):
        bxln = B("xln")
        bhq, bhqT = B("hq"), B("hqT")
        bwgT, bwuT, bwdT = B("wgT"), B("wuT"), B("wdT")
        pC, pD = B("psC"), B("psD")
        slices = [(0, 512), (512, 512), (1024, 384), (1408, 512), (1920, 512), (2432, 384)]

        def xqt_of(t):
            if t % 2 == 0:
                return xq, B("xq"), xqT, B("xqT")
            return xq2, B("xq2"), xqT2, B("xqT2")

        def front(t):
            xb = xbuf if t % 2 == 0 else xbuf2
            bx = B("xbuf" if t % 2 == 0 else "xbuf2")
            xqv, bxq, xqTv, bxqT = xqt_of(t)
            rd = [B(src_dep + "_" + str(t))] if src_dep else []
            dma("sp", xb[:], src_h[t * 128:(t + 1) * 128, :], reads=rd, writes=[bx],
                stream="xio" if t % 2 == 0 else "xio2")
            vo("dve", lambda e, tt=t, x_=xb: e.tensor_scalar(xln[:], x_[:],
                meanA[:, tt:tt + 1], rstdA[:, tt:tt + 1], ALU.subtract, ALU.mult),
                reads=[bx, B("rstdA"), B("meanA_" + str(t))], writes=[bxln])
            quant(xln[:, 0:512], bxln, 512, 0, xqv[:, 0:512], bxq, "writes")
            quant(xln[:, 512:1024], bxln, 512, 1, xqv[:, 512:1024], bxq, "append")
            for kc in range(NKC):
                tap, tb = trans(xqv[:, ts32(kc)], reads=[bxq],
                                slot=("psT", "psG")[kc % 2])
                act(xqTv[:, ts32(kc)], tap, AF.Copy,
                    reads=[tb], writes=[bxqT] if kc == 0 else (),
                    append=() if kc == 0 else [bxqT])

        front(0)
        for t in range(NT):
            xb = xbuf if t % 2 == 0 else xbuf2
            bx = B("xbuf" if t % 2 == 0 else "xbuf2")
            xqv, bxq, xqTv, bxqT = xqt_of(t)
            for si, (off, wdt) in enumerate(slices):
                gbank, gname = (ps["A"], "psA") if si % 2 == 0 else (ps["E"], "psE")
                ubank, uname = (ps["B"], "psB") if si % 2 == 0 else (ps["F"], "psF")
                pG_, pU_ = B(gname), B(uname)
                for kc in range(NKC):
                    mm(gbank[:, 0:wdt], xqTv[:, ts32(kc)],
                       wgT[:, kc * I + off: kc * I + off + wdt],
                       start=(kc == 0), stop=(kc == NKC - 1),
                       reads=[bxqT, bwgT], writes=[pG_] if kc == 0 else (),
                       append=() if kc == 0 else [pG_])
                for kc in range(NKC):
                    mm(ubank[:, 0:wdt], xqTv[:, ts32(kc)],
                       wuT[:, kc * I + off: kc * I + off + wdt],
                       start=(kc == 0), stop=(kc == NKC - 1),
                       reads=[bxqT, bwuT], writes=[pU_] if kc == 0 else (),
                       append=() if kc == 0 else [pU_])
                seti = si % 2
                hb = hsl[seti]
                bhb = B("hsl%d" % seti)
                act(hb[:, 0:wdt], gbank[:, 0:wdt], AF.Silu, reads=[pG_], writes=[bhb])
                vo("dve", lambda e, b_=hb, u=ubank, w=wdt: e.tensor_tensor(
                    b_[:, 0:w], b_[:, 0:w], u[:, 0:w], ALU.mult),
                    reads=[bhb, pU_], writes=[bhb])
                quant(hb[:, 0:wdt], bhb, wdt, seti, hq[:, off:off + wdt], bhq,
                      "writes" if si == 0 else "append")
            if t + 1 < NT:
                front(t + 1)
            for kc in range(NKI):
                tap, tb = trans(hq[:, ts32(kc)], reads=[bhq],
                                slot=("psT", "psG", "psC", "psD")[kc % 4])
                act(hqT[:, ts32(kc)], tap, AF.Copy,
                    reads=[tb], writes=[bhqT] if kc == 0 else (),
                    append=() if kc == 0 else [bhqT])
            for kc in range(NKI):
                mm(ps["C"][:], hqT[:, ts32(kc)], wdT[:, kc * D: kc * D + 512],
                   start=(kc == 0), stop=(kc == NKI - 1), reads=[bhqT, bwdT],
                   writes=[pC] if kc == 0 else (), append=() if kc == 0 else [pC])
            for kc in range(NKI):
                mm(ps["D"][:], hqT[:, ts32(kc)], wdT[:, kc * D + 512: kc * D + 1024],
                   start=(kc == 0), stop=(kc == NKI - 1), reads=[bhqT, bwdT],
                   writes=[pD] if kc == 0 else (), append=() if kc == 0 else [pD])
            vo("dve", lambda e, x_=xb: e.scalar_tensor_tensor(x_[:, 0:512], ps["C"][:], 0.5,
                x_[:, 0:512], ALU.mult, ALU.add), reads=[pC], writes=[bx])
            vo("dve", lambda e, x_=xb: e.scalar_tensor_tensor(x_[:, 512:1024], ps["D"][:], 0.5,
                x_[:, 512:1024], ALU.mult, ALU.add), reads=[pD], append=[bx])
            dma("act", dst_h[t * 128:(t + 1) * 128, :], xb[:], reads=[bx],
                writes=[B(dst_tag + "_" + str(t))], stream="xio" if t % 2 == 0 else "xio2")

    def attn_a():
        import os
        KA = int(os.environ.get("KATTN", "99"))
        bx, bxln = B("xbuf"), B("xln")
        bxlnb, bxlnT = B("xq"), B("xqT")
        bkf, bvs, bqf = B("kfb"), B("vsb"), B("qfTb")
        pA, pB, pE, pF = B("psA"), B("psB"), B("psE"), B("psF")
        pG, pC = B("psG"), B("psC")
        bph1, bph2 = B("xln"), B("sgbuf")
        vo("dve", lambda e: e.memset(
            vsb[:].rearrange("p (h c) -> p h c", c=66)[:, :, 64:65], 1.0),
            reads=[], writes=[B("vones")])
        for t in range(NT):
            seq = t // SEQT
            first = (t % SEQT) == 0
            last = (t % SEQT) == SEQT - 1
            dma("sp", xbuf[:], x1_h[t * 128:(t + 1) * 128, :],
                reads=[B("x1_" + str(t))], writes=[bx], stream="xio")
            if os.environ.get("KV1"):
                vo("dve", lambda e: e.tensor_copy(xln[:], xbuf[:]), reads=[bx], writes=[bxln])
            else:
                vo("dve", lambda e, tt=t: e.tensor_scalar(xln[:], xbuf[:],
                    meanA[:, tt:tt + 1], rstdA[:, tt:tt + 1], ALU.subtract, ALU.mult),
                    reads=[bx, B("rstdA"), B("meanA_" + str(t))], writes=[bxln])
            vo("dve", lambda e: e.tensor_copy(xlnb[:], xln[:]), reads=[bxln], writes=[bxlnb])
            for kc in range(NKC):
                trans(xlnb[:, ts32(kc)], reads=[bxlnb] + ([bqf, bkf, bvs] if t > 0 else []))
                vo("dve", lambda e, kk=kc: e.tensor_copy(xlnT[:, ts32(kk)], psT[:]),
                    reads=[B("psT")], writes=[bxlnT] if kc == 0 else (),
                    append=() if kc == 0 else [bxlnT])
            if KA < 2:
                continue
            for h in range(H):
                par, j = h % 2, h // 2
                bank = ps["E"] if j < 4 else ps["F"]
                pb = pE if j < 4 else pF
                st = (par == 0 and j % 4 == 0)
                mm(bank[64 * par:64 * par + 64, 128 * (j % 4):128 * (j % 4) + 128],
                   wqs[:, h * DH:(h + 1) * DH], xlnT[:, ts32(h // 2)],
                   start=True, stop=True, reads=[B("wqs"), bxlnT],
                   writes=[pb] if st else (), append=() if st else [pb],
                   tp=(0, 64 * par))
            KPHI = int(os.environ.get("KPHI", "4"))
            def phi(dst_ap, dst_b, bankL, bankR, pbL, pbR):
                vo("dve", lambda e, s=bankL: e.tensor_single_scalar(
                    xln[:, 0:512], s[:], 0.0, ALU.min), reads=[pbL], writes=[bph1])
                vo("dve", lambda e, s=bankR: e.tensor_single_scalar(
                    xln[:, 512:1024], s[:], 0.0, ALU.min), reads=[pbR], append=[bph1])
                if KPHI >= 2:
                    if os.environ.get("KV4"):
                        act(sgbuf[:], xln[:], AF.Relu, reads=[bph1], writes=[bph2])
                    elif os.environ.get("KV5"):
                        act(sgbuf[:, 0:512], bankL[:], AF.Relu, reads=[pbL, bph1], writes=[bph2])
                        act(sgbuf[:, 512:1024], bankR[:], AF.Relu, reads=[pbR], append=[bph2])
                    else:
                        act(sgbuf[:, 0:512], bankL[:], AF.Relu, reads=[pbL], writes=[bph2])
                        act(sgbuf[:, 512:1024], bankR[:], AF.Relu, reads=[pbR], append=[bph2])
                if KPHI >= 3:
                    act(xln[:], xln[:], AF.Exp, reads=[bph1], writes=[bph1])
                if KPHI >= 4:
                    vo("dve", lambda e, d=dst_ap: e.tensor_tensor(d, xln[:], sgbuf[:], ALU.add),
                        reads=[bph1, bph2], writes=[dst_b])
                else:
                    vo("dve", lambda e, d=dst_ap: e.tensor_copy(d, xln[:]),
                        reads=[bph1], writes=[dst_b])
            if KA < 3:
                continue
            phi(qfTb[:], bqf, ps["E"], ps["F"], pE, pF)
            KQF = int(os.environ.get("KQF", "2"))
            if KQF == 1:
                dma("sp", qfT_h[t], qfTb[:], reads=[bqf], writes=[B("qfT_" + str(t))], stream="qfT")
            elif KQF == 2:
                dma("act", qfT_h[t], qfTb[:], reads=[bqf], writes=[B("qfT_" + str(t))], stream="qfT")
            if KA < 4:
                continue
            for h in range(H):
                bank = ps["A"] if h < 8 else ps["B"]
                pb = pA if h < 8 else pB
                st = (h % 8 == 0)
                mm(bank[:, 64 * (h % 8):64 * (h % 8) + 64], xlnT[:, ts32(h // 2)],
                   wks[:, h * DH:(h + 1) * DH], start=True, stop=True,
                   reads=[B("wks"), bxlnT], writes=[pb] if st else (),
                   append=() if st else [pb])
            phi(kf[:], bkf, ps["A"], ps["B"], pA, pB)
            for h in range(H):
                bank = ps["A"] if h < 8 else ps["B"]
                pb = pA if h < 8 else pB
                st = (h % 8 == 0)
                mm(bank[:, 64 * (h % 8):64 * (h % 8) + 64], xlnT[:, ts32(h // 2)],
                   wvs[:, h * DH:(h + 1) * DH], start=True, stop=True,
                   reads=[B("wvs"), bxlnT], writes=[pb] if st else (),
                   append=() if st else [pb])
            for bi, (bank, pb) in enumerate(((ps["A"], pA), (ps["B"], pB))):
                vo("dve", lambda e, b=bank, off=bi * 8: e.tensor_copy(
                    vsb[:].rearrange("p (h c) -> p h c", c=66)[:, off:off + 8, 0:64],
                    b[:].rearrange("p (h c) -> p h c", c=64)),
                    reads=[pb], writes=[bvs] if bi == 0 else (),
                    append=() if bi == 0 else [bvs])
            if KA < 6:
                continue
            pD2 = B("psD")
            for h in range(H):
                if h < 7:
                    bank, pb, slot = ps["G"], pG, h
                elif h < 14:
                    bank, pb, slot = ps["C"], pC, h - 7
                else:
                    bank, pb, slot = ps["D"], pD2, h - 14
                st = (first and slot == 0)
                mm(bank[0:64, 65 * slot:65 * slot + 65],
                   kf[:, h * DH:(h + 1) * DH], vsb[:, h * 66:h * 66 + 65],
                   start=first, stop=last,
                   reads=[bkf, bvs, B("vones")],
                   writes=[pb] if st else (), append=() if st else [pb])
            if last:
                for bi, (bank, pb, nsl) in enumerate(((ps["G"], pG, 7), (ps["C"], pC, 7),
                                                      (ps["D"], pD2, 2))):
                    h0 = (0, 7, 14)[bi]
                    vo("dve", lambda e, b=bank, s=seq, hh0=h0, nn=nsl: e.tensor_copy(
                        kvsb[0:64, s * 1056 + 66 * hh0: s * 1056 + 66 * (hh0 + nn)].rearrange(
                            "p (s2 c) -> p s2 c", c=66)[:, :, 0:65],
                        b[0:64, 0:65 * nn].rearrange("p (s2 c) -> p s2 c", c=65)),
                        reads=[pb], writes=[B("kv_%d_%d" % (seq, bi))])

    def attn_b():
        import os
        KB = int(os.environ.get("KB", "9"))
        bx, bqf = B("xbuf"), B("qfTb")
        bctx, bctxT = B("qfTb"), B("kfb")
        bden, brden = B("dens"), B("rdens")
        pE, pF, pA = B("psE"), B("psF"), B("psA")
        pC, pD = B("psC"), B("psD")
        bwos = B("wos")
        for t in range(NT):
            seq = t // SEQT
            kvr = [B("kv_%d_%d" % (seq, bi)) for bi in range(3)]
            dma("sp", xbuf[:], x1_h[t * 128:(t + 1) * 128, :],
                reads=[B("x1_" + str(t))], writes=[bx], stream="xio")
            dma("act", qfTb2.ap().rearrange("d (b par c) -> d b par c", par=2, c=128),
                qfT_h[t].rearrange("(par d) (b c) -> d b par c", par=2, c=128),
                reads=[B("qfT_" + str(t))], writes=[bqf], stream="qfT")
            if KB < 2:
                dma("sp", x2_h[t * 128:(t + 1) * 128, :], xbuf[:], reads=[bx],
                    writes=[B("x2_" + str(t))], stream="xio")
                continue
            for h in range(H):
                par, j = h % 2, h // 2
                if h < 7:
                    bank, pb, slot = ps["E"], pE, h
                elif h < 14:
                    bank, pb, slot = ps["F"], pF, h - 7
                else:
                    bank, pb, slot = ps["A"], pA, h - 14
                mm(bank[:, slot * 65:slot * 65 + 65],
                   qfTb2[0:64, 128 * h:128 * h + 128],
                   kvsb[0:64, seq * 1056 + 66 * h: seq * 1056 + 66 * h + 65],
                   start=True, stop=True, reads=[bqf] + kvr,
                   writes=[pb] if slot == 0 else (), append=() if slot == 0 else [pb])
            if KB < 3:
                vo("dve", lambda e: e.tensor_copy(xbuf[:, 0:455], ps["E"][:, 0:455]),
                    reads=[pE], writes=[bx])
                vo("dve", lambda e: e.tensor_copy(xbuf[:, 455:910], ps["F"][:, 0:455]),
                    reads=[pF], append=[bx])
                vo("dve", lambda e: e.tensor_copy(xbuf[:, 910:1024], ps["A"][:, 0:114]),
                    reads=[pA], append=[bx])
                dma("sp", x2_h[t * 128:(t + 1) * 128, :], xbuf[:], reads=[bx],
                    writes=[B("x2_" + str(t))], stream="xio")
                continue
            vo("dve", lambda e: e.tensor_copy(dens[:, 0:7],
                ps["E"][:, 0:455].rearrange("p (s c) -> p s c", c=65)[:, :, 64]),
                reads=[pE], writes=[bden])
            vo("dve", lambda e: e.tensor_copy(dens[:, 7:14],
                ps["F"][:, 0:455].rearrange("p (s c) -> p s c", c=65)[:, :, 64]),
                reads=[pF], append=[bden])
            vo("dve", lambda e: e.tensor_copy(dens[:, 14:16],
                ps["A"][:, 0:130].rearrange("p (s c) -> p s c", c=65)[:, :, 64]),
                reads=[pA], append=[bden])
            vo("dve", lambda e: e.tensor_single_scalar(dens[:], dens[:], EPS_ATT, ALU.add),
                reads=[bden], writes=[bden])
            vo("dve", lambda e: e.reciprocal(rden[:], dens[:]), reads=[bden], writes=[brden, bden])
            if KB < 4:
                dma("sp", x2_h[t * 128:(t + 1) * 128, :], xbuf[:], reads=[bx],
                    writes=[B("x2_" + str(t))], stream="xio")
                continue
            for (bank, pb, h0, nh) in ((ps["E"], pE, 0, 7), (ps["F"], pF, 7, 7),
                                       (ps["A"], pA, 14, 2)):
                rb = rden[:, h0:h0 + nh].unsqueeze(2).broadcast_to([128, nh, 64])
                vo("dve", lambda e, b=bank, hh=h0, nn=nh, r=rb: e.tensor_tensor(
                    ctx[:, hh * 64:(hh + nn) * 64].rearrange("p (s c) -> p s c", c=64),
                    b[:, 0:nn * 65].rearrange("p (s c) -> p s c", c=65)[:, :, 0:64],
                    r, ALU.mult),
                    reads=[pb, brden], writes=[bctx] if h0 == 0 else (),
                    append=() if h0 == 0 else [bctx])
            if KB < 5:
                dma("sp", x2_h[t * 128:(t + 1) * 128, :], xbuf[:], reads=[bx],
                    writes=[B("x2_" + str(t))], stream="xio")
                continue
            for kc in range(NKC):
                trans(ctx[:, ts32(kc)], reads=[bctx])
                vo("dve", lambda e, kk=kc: e.tensor_copy(ctxT[:, ts32(kk)], psT[:]),
                    reads=[B("psT")], writes=[bctxT] if kc == 0 else (),
                    append=() if kc == 0 else [bctxT])
            for kc in range(NKC):
                mm(ps["C"][:], ctxT[:, ts32(kc)], wos[:, kc * D: kc * D + 512],
                   start=(kc == 0), stop=(kc == NKC - 1), reads=[bctxT, bwos],
                   writes=[pC] if kc == 0 else (), append=() if kc == 0 else [pC])
            for kc in range(NKC):
                mm(ps["D"][:], ctxT[:, ts32(kc)], wos[:, kc * D + 512: kc * D + 1024],
                   start=(kc == 0), stop=(kc == NKC - 1), reads=[bctxT, bwos],
                   writes=[pD] if kc == 0 else (), append=() if kc == 0 else [pD])
            vo("dve", lambda e: e.tensor_tensor(xbuf[:, 0:512], xbuf[:, 0:512],
                ps["C"][:], ALU.add), reads=[pC], writes=[bx], append=())
            vo("dve", lambda e: e.tensor_tensor(xbuf[:, 512:1024], xbuf[:, 512:1024],
                ps["D"][:], ALU.add), reads=[pD], append=[bx])
            dma("sp", x2_h[t * 128:(t + 1) * 128, :], xbuf[:], reads=[bx],
                writes=[B("x2_" + str(t))], stream="xio")

    import os
    NPH = int(os.environ.get("KPHASES", "99"))
    init_consts()
    dma("sp", ident[:], id_h, reads=[], writes=[B("ident")], stream="ident")
    if os.environ.get("KV2"):
        vo("pool", lambda e: e.memset(wqs[:], 0.01), reads=[], writes=[B("wqs")])
        vo("pool", lambda e: e.memset(wks[:], 0.01), reads=[], writes=[B("wks")])
        vo("pool", lambda e: e.memset(wvs[:], 0.01), reads=[], writes=[B("wvs")])
    else:
        dma("pool", wqs[:], wqp_h, reads=[], writes=[B("wqs")], stream="wq")
        dma("pool", wks[:], wkp_h, reads=[], writes=[B("wks")], stream="wk")
        dma("pool", wvs[:], wvp_h, reads=[], writes=[B("wvs")], stream="wv")

    def _setphase(nm):
        P.phase = nm

    phases = [
        lambda: (_setphase("p1pre"), ln_stats(x0_h, None)),
        lambda: (_setphase("p0"), weight_quant(wg_h, NKI, D, wgT, "wgT")),
        lambda: weight_quant(wu_h, NKI, D, wuT, "wuT"),
        lambda: weight_quant(wd_h, NKC, I, wdT, "wdT"),
        lambda: (_setphase("p1"), ffn_phase(x0_h, None, x1_h, "x1")),
        lambda: (_setphase("p2pre"), ln_stats(x1_h, "x1")),
        lambda: (_setphase("p2a"), attn_a()),
        lambda: (vo("pool", lambda e: e.nop(), reads=[],
                    writes=[B("hq"), B("hqT"), B("xq2"), B("xqT2"), B("wos")]),
                 dma("pool", wos.rearrange("p (c o) -> p c o", o=D),
                     wo_h.rearrange("(c p) o -> p c o", p=128),
                     reads=[], writes=[B("wos")], stream="wos"),
                 _setphase("p2b"), attn_b()),
        lambda: (_setphase("p3pre"), ln_stats(x2_h, "x2")),
        lambda: (vo("pool", lambda e: e.nop(), reads=[],
                    writes=[B("hq"), B("hqT"), B("xq2"), B("xqT2"), B("wos")]),
                 _setphase("p3"), ffn_phase(x2_h, "x2", out_h, "out")),
    ]
    PSET = os.environ.get("KPSET", "")
    if PSET:
        sel = set(int(x) for x in PSET.split(","))
        for i, ph in enumerate(phases):
            if i in sel:
                ph()
    else:
        for i, ph in enumerate(phases):
            if i < NPH:
                ph()

    final_counts = dict(P.counts)

    sems = {k: nc.alloc_semaphore("sem_" + k) for k in P.sem_keys}
    eng_map = {"pe": "tensor", "act": "scalar", "dve": "vector",
               "pool": "gpsimd", "sp": "sync"}

    with nc.Block() as block:
        def emit_for(eng_name):
            def body(eng):
                cur = None
                import contextlib
                ctxm = [contextlib.nullcontext()]
                for (waits, fn, skey, amt, phn) in P.ops[eng_name]:
                    if phn != cur:
                        ctxm[0].__exit__(None, None, None) if cur is not None else None
                        ctxm[0] = nc.named_scope(phn)
                        ctxm[0].__enter__()
                        cur = phn
                    for (wk, wv) in waits:
                        eng.wait_ge(sems[wk], wv)
                    fn(eng).then_inc(sems[skey], amt)
                if cur is not None:
                    ctxm[0].__exit__(None, None, None)
                if eng_name == "sp":
                    for k in P.sem_keys:
                        if final_counts[k] > 0:
                            eng.wait_ge(sems[k], final_counts[k])
            return body

        for eng_name, attr in eng_map.items():
            getattr(block, attr)(emit_for(eng_name))

    return nc


_CACHE = {}


def kernel(L0_attn_query, L0_attn_Wq, L0_attn_Wk, L0_attn_Wv, L0_attn_Wo,
           L0_ffn_w_gate, L0_ffn_w_up, L0_ffn_w_down):
    x = np.ascontiguousarray(np.asarray(L0_attn_query, np.float32)).reshape(16, 2048, 1024)
    shards = [np.ascontiguousarray(x[2 * c:2 * c + 2].reshape(TPC, D)) for c in range(NCORES)]

    def pad_heads(w):
        w = np.asarray(w, np.float32)
        p = np.zeros((128, H, DH), np.float32)
        for h in range(H):
            p[64 * (h % 2):64 * (h % 2) + 64, h, :] = w[h]
        return np.ascontiguousarray(p.reshape(128, H * DH))

    common = {
        "wg": np.ascontiguousarray(np.asarray(L0_ffn_w_gate, np.float32)),
        "wu": np.ascontiguousarray(np.asarray(L0_ffn_w_up, np.float32)),
        "wd": np.ascontiguousarray(np.asarray(L0_ffn_w_down, np.float32)),
        "wqp": pad_heads(L0_attn_Wq),
        "wkp": pad_heads(L0_attn_Wk),
        "wvp": pad_heads(L0_attn_Wv),
        "wo": np.ascontiguousarray(np.asarray(L0_attn_Wo, np.float32)),
        "ident": np.eye(128, dtype=ml_dtypes.bfloat16),
    }
    if "nc" not in _CACHE:
        _CACHE["nc"] = build_nc()
    nc = _CACHE["nc"]
    in_maps = [{"x0": shards[c], **common} for c in range(NCORES)]
    res = run_bass_kernel_spmd(nc, in_maps, core_ids=list(range(NCORES)))
    out = np.empty((16, 2048, 1024), np.float32)
    for c in range(NCORES):
        out[2 * c:2 * c + 2] = np.asarray(res.results[c]["xout"], np.float32).reshape(2, 2048, 1024)
    return out


# version 17 (final)
# speedup vs baseline: 1.1758x; 1.0009x over previous
import numpy as np
import ml_dtypes
import concourse.bass as bass
import concourse.mybir as mybir
from concourse.bass_utils import run_bass_kernel_spmd

F32 = mybir.dt.float32
BF16 = mybir.dt.bfloat16
AF = mybir.ActivationFunctionType
ALU = mybir.AluOpType
AX = mybir.AxisListType

NCORES = 8
TPC = 4096
NT = 32
SEQT = 16
D = 1024
I = 2816
HW = I // 2
NKC = D // 128
NKI = I // 128
H, DH = 16, 64
EPS_LN = 1e-5
EPS_ATT = 1e-6

ENGS = ("pe", "act", "dve", "pool", "sp")


class Buf:
    __slots__ = ("name", "writers", "readers")

    def __init__(self, name):
        self.name = name
        self.writers = {}
        self.readers = {}


class Prog:
    ENG_SEMS = ("pe", "act", "dve", "pool")

    def __init__(self, nc):
        self.nc = nc
        self.ops = {e: [] for e in ENGS}
        self.sem_keys = ["pe", "act", "dve", "pool"]
        self.counts = {k: 0 for k in self.sem_keys}
        self.waited = {e: {} for e in ENGS}
        self.bufs = {}
        self.drains = {k: [] for k in self.ENG_SEMS}
        self.phase = "init"

    def _drain_at_or_after(self, skey, cnt):
        import os
        if not os.environ.get("KDRAIN"):
            return cnt
        import bisect
        dl = self.drains[skey]
        i = bisect.bisect_left(dl, cnt)
        if i < len(dl):
            return dl[i]
        self.counts[skey] += 1
        dc = self.counts[skey]
        self.ops[skey].append(([], lambda e: e.drain(), skey, 1, self.phase))
        dl.append(dc)
        return dc

    def stream(self, name):
        k = "dma_" + name
        if k not in self.counts:
            self.sem_keys.append(k)
            self.counts[k] = 0
        return k

    def B(self, name):
        if name not in self.bufs:
            self.bufs[name] = Buf(name)
        return self.bufs[name]

    PSUM_BUFS = frozenset(["psA", "psB", "psC", "psD", "psE", "psF", "psG", "psT"])

    def op(self, eng, emit_fn, reads=(), writes=(), append=(), dma=None, stream=None):
        deps = []
        for b in reads:
            deps += list(b.writers.items())
            if b.name in self.PSUM_BUFS:
                deps += list(b.readers.items())
        for b in writes:
            deps += list(b.writers.items())
            deps += list(b.readers.items())
        waits = []
        for (skey, cnt) in deps:
            if cnt <= 0:
                continue
            if skey in self.ENG_SEMS and skey != eng:
                cnt = self._drain_at_or_after(skey, cnt)
            if self.waited[eng].get(skey, 0) >= cnt:
                continue
            self.waited[eng][skey] = cnt
            waits.append((skey, cnt))
        skey, amt = (eng, 1) if dma is None else (self.stream(stream), 16)
        self.counts[skey] += amt
        cnt = self.counts[skey]
        self.ops[eng].append((waits, emit_fn, skey, amt, self.phase))
        for b in reads:
            b.readers[skey] = cnt
        for b in writes:
            b.writers = {skey: cnt}
            b.readers = {}
        for b in append:
            b.writers[skey] = cnt


def ts32(i):
    return slice(i * 128, (i + 1) * 128)


def build_nc():
    import os
    nc = bass.Bass("TRN2", target_bir_lowering=False)

    x0_h = nc.dram_tensor("x0", [TPC, D], F32, kind="ExternalInput").ap()
    wg_h = nc.dram_tensor("wg", [I, D], F32, kind="ExternalInput").ap()
    wu_h = nc.dram_tensor("wu", [I, D], F32, kind="ExternalInput").ap()
    wd_h = nc.dram_tensor("wd", [D, I], F32, kind="ExternalInput").ap()
    wqp_h = nc.dram_tensor("wqp", [128, H * DH], F32, kind="ExternalInput").ap()
    wkp_h = nc.dram_tensor("wkp", [128, H * DH], F32, kind="ExternalInput").ap()
    wvp_h = nc.dram_tensor("wvp", [128, H * DH], F32, kind="ExternalInput").ap()
    wo_h = nc.dram_tensor("wo", [D, D], F32, kind="ExternalInput").ap()
    id_h = nc.dram_tensor("ident", [128, 128], BF16, kind="ExternalInput").ap()
    out_h = nc.dram_tensor("xout", [TPC, D], F32, kind="ExternalOutput").ap()
    x1_h = nc.dram_tensor("x1scr", [TPC, D], F32, kind="Internal").ap()
    x2_h = nc.dram_tensor("x2scr", [TPC, D], F32, kind="Internal").ap()
    qfT_h = nc.dram_tensor("qfTscr", [NT, 128, D], BF16, kind="Internal").ap()

    wgT = nc.alloc_sbuf_tensor("wgT", [128, NKC * I], BF16)
    wuT = nc.alloc_sbuf_tensor("wuT", [128, NKC * I], BF16)
    wdT = nc.alloc_sbuf_tensor("wdT", [128, NKI * D], BF16)
    ident = nc.alloc_sbuf_tensor("idents", [128, 128], BF16)
    meanA = nc.alloc_sbuf_tensor("meanA", [128, NT], F32)
    varA = nc.alloc_sbuf_tensor("varA", [128, NT], F32)
    rstdA = nc.alloc_sbuf_tensor("rstdA", [128, NT], F32)
    st12 = nc.alloc_sbuf_tensor("st12", [128, 12], F32)
    mvtmp = nc.alloc_sbuf_tensor("mvtmp", [128, 2], F32)
    xbuf = nc.alloc_sbuf_tensor("xbuf", [128, D], F32)
    xln = nc.alloc_sbuf_tensor("xln", [128, D], F32)
    sgbuf = nc.alloc_sbuf_tensor("sgbuf", [128, D], F32)
    xbuf2 = nc.alloc_sbuf_tensor("xbuf2", [128, D], F32)
    hsl = [nc.alloc_sbuf_tensor("hsl%d" % i, [128, 512], F32) for i in range(2)]
    qsets = []
    for i in range(2):
        qsets.append({
            "w2": nc.alloc_sbuf_tensor("w2_%d" % i, [128, 512], F32),
            "g": nc.alloc_sbuf_tensor("gq_%d" % i, [128, 512], BF16),
            "m": [nc.alloc_sbuf_tensor("m%d_%d" % (j, i), [128, 512], BF16) for j in range(4)],
            "bm": nc.alloc_sbuf_tensor("bmq_%d" % i, [128, 32], F32),
            "s6": nc.alloc_sbuf_tensor("s6q_%d" % i, [128, 32], F32),
            "rs2": nc.alloc_sbuf_tensor("rs2q_%d" % i, [128, 32], F32),
        })
    xq = nc.alloc_sbuf_tensor("xq", [128, D], BF16)
    xqT = nc.alloc_sbuf_tensor("xqT", [128, D], BF16)
    arena = nc.alloc_sbuf_tensor("arena", [128, NKC * D], BF16)
    hq = arena[:, 0:I]
    hqT = arena[:, I:2 * I]
    xq2 = arena[:, 2 * I:2 * I + D]
    xqT2 = arena[:, 2 * I + D:2 * I + 2 * D]
    wos = arena[:]
    wqs = nc.alloc_sbuf_tensor("wqs", [128, H * DH], BF16)
    wks = nc.alloc_sbuf_tensor("wks", [128, H * DH], BF16)
    wvs = nc.alloc_sbuf_tensor("wvs", [128, H * DH], BF16)
    xlnb = xq
    xlnT = xqT
    kf = nc.alloc_sbuf_tensor("kfb", [128, H * DH], BF16)
    vsb = nc.alloc_sbuf_tensor("vsb", [128, H * 66], BF16)
    qfTb = nc.alloc_sbuf_tensor("qfTb", [128, D], BF16)
    kvsb = nc.alloc_sbuf_tensor("kvsb", [64, 2 * 1056], BF16)
    dens = nc.alloc_sbuf_tensor("dens", [128, H], F32)
    rden = dens
    ctx = qfTb
    ctxT = kf
    consts = nc.alloc_sbuf_tensor("consts", [128, 12], F32)
    qfTb2 = nc.alloc_sbuf_tensor("qfTb2", [64, 2048], BF16)

    ps = {}
    for nm in ("A", "B", "C", "D", "E", "F", "G"):
        ps[nm] = nc.alloc_psum_tensor("ps" + nm, [128, 512], F32)
    psT = nc.alloc_psum_tensor("psT", [128, 128], BF16)

    P = Prog(nc)
    B = P.B

    def dma(q, out_ap, in_ap, reads=(), writes=(), append=(), stream=None):
        assert stream is not None
        P.op(q, lambda e, o=out_ap, i=in_ap: e.dma_start(out=o, in_=i),
             reads=reads, writes=writes, append=append, dma=q, stream=stream)

    def mm(out_ap, lhsT_ap, rhs_ap, start, stop, reads, writes=(), append=(), tp=None):
        P.op("pe", lambda e, o=out_ap, l=lhsT_ap, r=rhs_ap, s1=start, s2=stop, t=tp:
             e.matmul(o, l, r, start=s1, stop=s2, tile_position=t),
             reads=reads, writes=writes, append=append)

    TSLOT = {
        "psT": psT[:],
        "psG": ps["G"][:, 0:64].bitcast(BF16),
        "psE": ps["E"][:, 0:64].bitcast(BF16),
        "psF": ps["F"][:, 0:64].bitcast(BF16),
        "psC": ps["C"][:, 0:64].bitcast(BF16),
        "psD": ps["D"][:, 0:64].bitcast(BF16),
    }

    def trans(in_ap, reads, slot="psT"):
        P.op("pe", lambda e, i=in_ap, sl=slot: e.transpose(TSLOT[sl], i, ident[:]),
             reads=list(reads) + [B("ident")], writes=[B(slot)])
        return TSLOT[slot], B(slot)

    def act(out_ap, in_ap, func, reads, writes=(), append=(), bias=None, scale=None):
        kw = {}
        if bias is not None:
            kw["bias"] = bias
        if scale is not None:
            kw["scale"] = scale
        P.op("act", lambda e, o=out_ap, i=in_ap, f=func, k=dict(kw): e.activation(o, i, f, **k),
             reads=reads, writes=writes, append=append)

    def vo(eng, fn, reads, writes=(), append=()):
        P.op(eng, fn, reads=reads, writes=writes, append=append)

    CONSTV = {-0.0625: 0, -0.5625: 1, -1.5625: 2, -3.0625: 3, EPS_LN: 4, -0.5: 5, -6.25: 6, -12.25: 7, -25.0: 8}

    def cbias(v):
        return consts[:, CONSTV[v]:CONSTV[v] + 1]

    def init_consts():
        for v, i in CONSTV.items():
            vo("pool", lambda e, vv=v, ii=i: e.memset(consts[:, ii:ii + 1], vv),
               reads=[], append=[B("consts")])

    def quant(src_ap, src_buf, W, seti, out_ap, out_wr, out_mode="writes"):
        q = qsets[seti]
        nb = W // 16
        sfx = "_%d" % seti
        bw2, bg = B("w2" + sfx), B("gq" + sfx)
        bbm, bs6, brs2 = B("bm" + sfx), B("s6" + sfx), B("rs2" + sfx)
        bmk = [B("m%d%s" % (j, sfx)) for j in range(4)]
        m = q["m"]
        act(q["g"][:, 0:W], src_ap, AF.Sign, reads=[src_buf], writes=[bg])
        act(q["w2"][:, 0:W], src_ap, AF.Square, reads=[src_buf], writes=[bw2])
        vo("dve", lambda e, sp=src_ap, n=nb: e.tensor_reduce(
            q["bm"][:, 0:n], sp.rearrange("p (n b) -> p n b", b=16), AX.X, ALU.max,
            apply_absolute_value=True), reads=[src_buf], writes=[bbm])
        vo("dve", lambda e, n=nb: e.tensor_scalar(q["s6"][:, 0:n], q["bm"][:, 0:n],
            1.0 / 6.0, 1e-12, ALU.mult, ALU.max), reads=[bbm], writes=[bs6])
        vo("dve", lambda e, n=nb: e.tensor_tensor(q["bm"][:, 0:n], q["s6"][:, 0:n],
            q["s6"][:, 0:n], ALU.mult), reads=[bs6], writes=[bbm])
        vo("dve", lambda e, n=nb: e.reciprocal(q["rs2"][:, 0:n], q["bm"][:, 0:n]),
            reads=[bbm], writes=[brs2])
        rsb = q["rs2"][:, 0:nb].unsqueeze(2).broadcast_to([128, nb, 16])
        vo("dve", lambda e, n=nb, r=rsb: e.tensor_tensor(
            q["w2"][:, 0:n * 16].rearrange("p (n b) -> p n b", b=16),
            q["w2"][:, 0:n * 16].rearrange("p (n b) -> p n b", b=16), r, ALU.mult),
            reads=[bw2, brs2], writes=[bw2])
        act(m[0][:, 0:W], q["w2"][:, 0:W], AF.Sign, bias=cbias(-0.0625),
            reads=[bw2, B("consts")], writes=[bmk[0]])
        act(m[1][:, 0:W], q["w2"][:, 0:W], AF.Sign, bias=cbias(-0.5625),
            reads=[bw2, B("consts")], writes=[bmk[1]])
        act(m[2][:, 0:W], q["w2"][:, 0:W], AF.Sign, bias=cbias(-1.5625),
            reads=[bw2, B("consts")], writes=[bmk[2]])
        act(m[3][:, 0:W], q["w2"][:, 0:W], AF.Sign, bias=cbias(-3.0625),
            reads=[bw2, B("consts")], writes=[bmk[3]])
        qeng = "pool" if os.environ.get("KPOOL") else "dve"
        vo(qeng, lambda e, W_=W: e.tensor_tensor(m[0][:, 0:W_], m[0][:, 0:W_],
            m[1][:, 0:W_], ALU.add), reads=[bmk[0], bmk[1]], writes=[bmk[0]])
        vo(qeng, lambda e, W_=W: e.tensor_tensor(m[2][:, 0:W_], m[2][:, 0:W_],
            m[3][:, 0:W_], ALU.add), reads=[bmk[2], bmk[3]], writes=[bmk[2]])
        vo(qeng, lambda e, W_=W: e.tensor_tensor(m[0][:, 0:W_], m[0][:, 0:W_],
            m[2][:, 0:W_], ALU.add), reads=[bmk[0], bmk[2]], writes=[bmk[0]])
        act(m[1][:, 0:W], q["w2"][:, 0:W], AF.Sign, bias=cbias(-6.25),
            reads=[bw2, B("consts")], writes=[bmk[1]])
        act(m[2][:, 0:W], q["w2"][:, 0:W], AF.Sign, bias=cbias(-12.25),
            reads=[bw2, B("consts")], writes=[bmk[2]])
        act(m[3][:, 0:W], q["w2"][:, 0:W], AF.Sign, bias=cbias(-25.0),
            reads=[bw2, B("consts")], writes=[bmk[3]])
        vo("dve", lambda e, W_=W: e.tensor_tensor(m[1][:, 0:W_], m[1][:, 0:W_],
            m[2][:, 0:W_], ALU.add), reads=[bmk[1], bmk[2]], writes=[bmk[1]])
        vo("dve", lambda e, W_=W: e.scalar_tensor_tensor(m[1][:, 0:W_], m[1][:, 0:W_],
            0.5, m[3][:, 0:W_], ALU.mult, ALU.add), reads=[bmk[1], bmk[3]], writes=[bmk[1]])
        vo("dve", lambda e, W_=W: e.scalar_tensor_tensor(m[0][:, 0:W_], m[0][:, 0:W_],
            0.25, m[1][:, 0:W_], ALU.mult, ALU.add), reads=[bmk[0], bmk[1]], writes=[bmk[0]])
        vo("dve", lambda e, W_=W: e.scalar_tensor_tensor(m[0][:, 0:W_], m[0][:, 0:W_],
            3.0, q["g"][:, 0:W_], ALU.add, ALU.mult), reads=[bmk[0], bg], writes=[bmk[0]])
        s6b = q["s6"][:, 0:nb].unsqueeze(2).broadcast_to([128, nb, 16])
        kw = {"writes": [out_wr]} if out_mode == "writes" else {"append": [out_wr]}
        vo("dve", lambda e, o=out_ap, n=nb, sb=s6b: e.tensor_tensor(
            o.rearrange("p (n b) -> p n b", b=16),
            m[0][:, 0:n * 16].rearrange("p (n b) -> p n b", b=16), sb, ALU.mult),
            reads=[bmk[0], bs6], **kw)

    def quant_dve(src_ap, src_buf, W, seti, out_ap, out_wr, out_mode="writes"):
        q = qsets[seti]
        nb = W // 16
        sfx = "_%d" % seti
        bw2, bg = B("w2" + sfx), B("gq" + sfx)
        bbm, bs6, brs2 = B("bm" + sfx), B("s6" + sfx), B("rs2" + sfx)
        bmk = [B("m%d%s" % (j, sfx)) for j in range(4)]
        m = q["m"]
        act(q["g"][:, 0:W], src_ap, AF.Sign, reads=[src_buf], writes=[bg])
        vo("dve", lambda e, sp=src_ap, W_=W: e.tensor_tensor(q["w2"][:, 0:W_], sp, sp, ALU.mult),
            reads=[src_buf], writes=[bw2])
        vo("dve", lambda e, sp=src_ap, n=nb: e.tensor_reduce(
            q["bm"][:, 0:n], sp.rearrange("p (n b) -> p n b", b=16), AX.X, ALU.max,
            apply_absolute_value=True), reads=[src_buf], writes=[bbm])
        vo("dve", lambda e, n=nb: e.tensor_scalar(q["s6"][:, 0:n], q["bm"][:, 0:n],
            1.0 / 6.0, 1e-12, ALU.mult, ALU.max), reads=[bbm], writes=[bs6])
        vo("dve", lambda e, n=nb: e.tensor_tensor(q["bm"][:, 0:n], q["s6"][:, 0:n],
            q["s6"][:, 0:n], ALU.mult), reads=[bs6], writes=[bbm])
        vo("dve", lambda e, n=nb: e.reciprocal(q["rs2"][:, 0:n], q["bm"][:, 0:n]),
            reads=[bbm], writes=[brs2])
        rsb = q["rs2"][:, 0:nb].unsqueeze(2).broadcast_to([128, nb, 16])
        vo("dve", lambda e, n=nb, r=rsb: e.tensor_tensor(
            q["w2"][:, 0:n * 16].rearrange("p (n b) -> p n b", b=16),
            q["w2"][:, 0:n * 16].rearrange("p (n b) -> p n b", b=16), r, ALU.mult),
            reads=[bw2, brs2], writes=[bw2])
        vo("dve", lambda e, W_=W: e.tensor_scalar(m[0][:, 0:W_], q["w2"][:, 0:W_],
            0.0625, 0.5, ALU.is_gt, ALU.mult), reads=[bw2], writes=[bmk[0]])
        vo("dve", lambda e, W_=W: e.tensor_scalar(m[1][:, 0:W_], q["w2"][:, 0:W_],
            0.5625, 0.5, ALU.is_gt, ALU.mult), reads=[bw2], writes=[bmk[1]])
        vo("dve", lambda e, W_=W: e.tensor_tensor(m[0][:, 0:W_], m[0][:, 0:W_],
            m[1][:, 0:W_], ALU.add), reads=[bmk[0], bmk[1]], writes=[bmk[0]])
        vo("dve", lambda e, W_=W: e.tensor_scalar(m[2][:, 0:W_], q["w2"][:, 0:W_],
            1.5625, 0.5, ALU.is_gt, ALU.mult), reads=[bw2], writes=[bmk[2]])
        vo("dve", lambda e, W_=W: e.tensor_scalar(m[3][:, 0:W_], q["w2"][:, 0:W_],
            3.0625, 0.5, ALU.is_gt, ALU.mult), reads=[bw2], writes=[bmk[3]])
        vo("dve", lambda e, W_=W: e.tensor_tensor(m[2][:, 0:W_], m[2][:, 0:W_],
            m[3][:, 0:W_], ALU.add), reads=[bmk[2], bmk[3]], writes=[bmk[2]])
        vo("dve", lambda e, W_=W: e.tensor_tensor(m[0][:, 0:W_], m[0][:, 0:W_],
            m[2][:, 0:W_], ALU.add), reads=[bmk[0], bmk[2]], writes=[bmk[0]])
        vo("dve", lambda e, W_=W: e.tensor_single_scalar(m[1][:, 0:W_], q["w2"][:, 0:W_],
            6.25, ALU.is_gt), reads=[bw2], writes=[bmk[1]])
        vo("dve", lambda e, W_=W: e.tensor_scalar(m[3][:, 0:W_], q["w2"][:, 0:W_],
            25.0, 2.0, ALU.is_gt, ALU.mult), reads=[bw2], writes=[bmk[3]])
        vo("dve", lambda e, W_=W: e.tensor_tensor(m[1][:, 0:W_], m[1][:, 0:W_],
            m[3][:, 0:W_], ALU.add), reads=[bmk[1], bmk[3]], writes=[bmk[1]])
        vo("dve", lambda e, W_=W: e.tensor_single_scalar(m[3][:, 0:W_], q["w2"][:, 0:W_],
            12.25, ALU.is_gt), reads=[bw2], writes=[bmk[3]])
        vo("dve", lambda e, W_=W: e.tensor_tensor(m[1][:, 0:W_], m[1][:, 0:W_],
            m[3][:, 0:W_], ALU.add), reads=[bmk[1], bmk[3]], writes=[bmk[1]])
        vo("dve", lambda e, W_=W: e.tensor_tensor(m[0][:, 0:W_], m[0][:, 0:W_],
            m[1][:, 0:W_], ALU.add), reads=[bmk[0], bmk[1]], writes=[bmk[0]])
        vo("dve", lambda e, W_=W: e.tensor_tensor(m[0][:, 0:W_], m[0][:, 0:W_],
            q["g"][:, 0:W_], ALU.mult), reads=[bmk[0], bg], writes=[bmk[0]])
        s6b = q["s6"][:, 0:nb].unsqueeze(2).broadcast_to([128, nb, 16])
        kw = {"writes": [out_wr]} if out_mode == "writes" else {"append": [out_wr]}
        vo("dve", lambda e, o=out_ap, n=nb, sb=s6b: e.tensor_tensor(
            o.rearrange("p (n b) -> p n b", b=16),
            m[0][:, 0:n * 16].rearrange("p (n b) -> p n b", b=16), sb, ALU.mult),
            reads=[bmk[0], bs6], **kw)

    _quant_orig = quant

    def quant_mix(src_ap, src_buf, W, seti, out_ap, out_wr, out_mode="writes"):
        fn = quant_dve if seti == 0 else _quant_orig
        fn(src_ap, src_buf, W, seti, out_ap, out_wr, out_mode)

    quant = quant_mix

    def ln_stats(src_h, dep_tiles):
        bx, bst, bmv = B("xbuf"), B("st12"), B("mvtmp")
        for t in range(NT):
            rd = [B(dep_tiles + "_" + str(t))] if dep_tiles else []
            dma("sp", xbuf[:], src_h[t * 128:(t + 1) * 128, :], reads=rd, writes=[bx], stream="xio")
            vo("dve", lambda e: e.bn_stats(st12[:, 0:6], xbuf[:, 0:512]),
                reads=[bx], writes=[bst])
            vo("dve", lambda e: e.bn_stats(st12[:, 6:12], xbuf[:, 512:1024]),
                reads=[bx], append=[bst])
            vo("dve", lambda e: e.bn_aggr(mvtmp[:], st12[:]), reads=[bst], writes=[bmv])
            vo("dve", lambda e, tt=t: e.tensor_copy(meanA[:, tt:tt + 1], mvtmp[:, 0:1]),
                reads=[bmv], writes=[B("meanA_" + str(t))])
            vo("dve", lambda e, tt=t: e.tensor_copy(varA[:, tt:tt + 1], mvtmp[:, 1:2]),
                reads=[bmv], writes=[B("varA_" + str(t))])
        rd = [B("varA_" + str(t)) for t in range(NT)]
        act(rstdA[:], varA[:], AF.Ln, bias=cbias(EPS_LN), reads=rd + [B("consts")], writes=[B("lnvar")])
        act(rstdA[:], rstdA[:], AF.Exp, scale=cbias(-0.5), reads=[B("lnvar"), B("consts")], writes=[B("rstdA")])

    def weight_quant(w_h, n_row_tiles, F, dstT, tag, tslots=("psT", "psG")):
        bwq = B("hq")
        width = n_row_tiles * 128
        nkc = F // 128
        chunks = [(c, min(512, F - c)) for c in range(0, F, 512)]
        for r in range(n_row_tiles):
            for ci, (off, wdt) in enumerate(chunks):
                seti = ci % 2
                hb = hsl[seti]
                bhb = B("hsl%d" % seti)
                dma("sp", hb[:, 0:wdt], w_h[r * 128:(r + 1) * 128, off:off + wdt],
                    reads=[], writes=[bhb], stream="hload%d" % seti)
                quant_dve(hb[:, 0:wdt], bhb, wdt, seti, hq[:, off:off + wdt], bwq,
                          "writes" if ci == 0 else "append")
            for kc in range(nkc):
                tap, tb = trans(hq[:, ts32(kc)], reads=[bwq], slot=tslots[kc % len(tslots)])
                vo("dve", lambda e, kk=kc, rr=r, dd=dstT, w=width, t=tap: e.tensor_copy(
                    dd[:, kk * w + rr * 128: kk * w + rr * 128 + 128], t),
                    reads=[tb], append=[B(tag)])

    def ffn_phase(src_h, src_dep, dst_h, dst_tag):
        bxln = B("xln")
        bhq, bhqT = B("hq"), B("hqT")
        bwgT, bwuT, bwdT = B("wgT"), B("wuT"), B("wdT")
        pC, pD = B("psC"), B("psD")
        slices = [(0, 512), (512, 512), (1024, 384), (1408, 512), (1920, 512), (2432, 384)]

        def xqt_of(t):
            if t % 2 == 0:
                return xq, B("xq"), xqT, B("xqT")
            return xq2, B("xq2"), xqT2, B("xqT2")

        def front(t):
            xb = xbuf if t % 2 == 0 else xbuf2
            bx = B("xbuf" if t % 2 == 0 else "xbuf2")
            xqv, bxq, xqTv, bxqT = xqt_of(t)
            rd = [B(src_dep + "_" + str(t))] if src_dep else []
            dma("sp", xb[:], src_h[t * 128:(t + 1) * 128, :], reads=rd, writes=[bx],
                stream="xio" if t % 2 == 0 else "xio2")
            vo("dve", lambda e, tt=t, x_=xb: e.tensor_scalar(xln[:], x_[:],
                meanA[:, tt:tt + 1], rstdA[:, tt:tt + 1], ALU.subtract, ALU.mult),
                reads=[bx, B("rstdA"), B("meanA_" + str(t))], writes=[bxln])
            quant(xln[:, 0:512], bxln, 512, 0, xqv[:, 0:512], bxq, "writes")
            quant(xln[:, 512:1024], bxln, 512, 1, xqv[:, 512:1024], bxq, "append")
            for kc in range(NKC):
                tap, tb = trans(xqv[:, ts32(kc)], reads=[bxq],
                                slot=("psT", "psG")[kc % 2])
                act(xqTv[:, ts32(kc)], tap, AF.Copy,
                    reads=[tb], writes=[bxqT] if kc == 0 else (),
                    append=() if kc == 0 else [bxqT])

        front(0)
        for t in range(NT):
            xb = xbuf if t % 2 == 0 else xbuf2
            bx = B("xbuf" if t % 2 == 0 else "xbuf2")
            xqv, bxq, xqTv, bxqT = xqt_of(t)
            for si, (off, wdt) in enumerate(slices):
                gbank, gname = (ps["A"], "psA") if si % 2 == 0 else (ps["E"], "psE")
                ubank, uname = (ps["B"], "psB") if si % 2 == 0 else (ps["F"], "psF")
                pG_, pU_ = B(gname), B(uname)
                for kc in range(NKC):
                    mm(gbank[:, 0:wdt], xqTv[:, ts32(kc)],
                       wgT[:, kc * I + off: kc * I + off + wdt],
                       start=(kc == 0), stop=(kc == NKC - 1),
                       reads=[bxqT, bwgT], writes=[pG_] if kc == 0 else (),
                       append=() if kc == 0 else [pG_])
                for kc in range(NKC):
                    mm(ubank[:, 0:wdt], xqTv[:, ts32(kc)],
                       wuT[:, kc * I + off: kc * I + off + wdt],
                       start=(kc == 0), stop=(kc == NKC - 1),
                       reads=[bxqT, bwuT], writes=[pU_] if kc == 0 else (),
                       append=() if kc == 0 else [pU_])
                seti = si % 2
                hb = hsl[seti]
                bhb = B("hsl%d" % seti)
                act(hb[:, 0:wdt], gbank[:, 0:wdt], AF.Silu, reads=[pG_], writes=[bhb])
                vo("dve", lambda e, b_=hb, u=ubank, w=wdt: e.tensor_tensor(
                    b_[:, 0:w], b_[:, 0:w], u[:, 0:w], ALU.mult),
                    reads=[bhb, pU_], writes=[bhb])
                quant(hb[:, 0:wdt], bhb, wdt, seti, hq[:, off:off + wdt], bhq,
                      "writes" if si == 0 else "append")
            if t + 1 < NT:
                front(t + 1)
            for kc in range(NKI):
                tap, tb = trans(hq[:, ts32(kc)], reads=[bhq],
                                slot=("psT", "psG", "psC", "psD")[kc % 4])
                act(hqT[:, ts32(kc)], tap, AF.Copy,
                    reads=[tb], writes=[bhqT] if kc == 0 else (),
                    append=() if kc == 0 else [bhqT])
            for kc in range(NKI):
                mm(ps["C"][:], hqT[:, ts32(kc)], wdT[:, kc * D: kc * D + 512],
                   start=(kc == 0), stop=(kc == NKI - 1), reads=[bhqT, bwdT],
                   writes=[pC] if kc == 0 else (), append=() if kc == 0 else [pC])
            for kc in range(NKI):
                mm(ps["D"][:], hqT[:, ts32(kc)], wdT[:, kc * D + 512: kc * D + 1024],
                   start=(kc == 0), stop=(kc == NKI - 1), reads=[bhqT, bwdT],
                   writes=[pD] if kc == 0 else (), append=() if kc == 0 else [pD])
            vo("dve", lambda e, x_=xb: e.scalar_tensor_tensor(x_[:, 0:512], ps["C"][:], 0.5,
                x_[:, 0:512], ALU.mult, ALU.add), reads=[pC], writes=[bx])
            vo("dve", lambda e, x_=xb: e.scalar_tensor_tensor(x_[:, 512:1024], ps["D"][:], 0.5,
                x_[:, 512:1024], ALU.mult, ALU.add), reads=[pD], append=[bx])
            dma("act", dst_h[t * 128:(t + 1) * 128, :], xb[:], reads=[bx],
                writes=[B(dst_tag + "_" + str(t))], stream="xio" if t % 2 == 0 else "xio2")

    def attn_a():
        import os
        KA = int(os.environ.get("KATTN", "99"))
        bx, bxln = B("xbuf"), B("xln")
        bxlnb, bxlnT = B("xq"), B("xqT")
        bkf, bvs, bqf = B("kfb"), B("vsb"), B("qfTb")
        pA, pB, pE, pF = B("psA"), B("psB"), B("psE"), B("psF")
        pG, pC = B("psG"), B("psC")
        bph1, bph2 = B("xln"), B("sgbuf")
        vo("dve", lambda e: e.memset(
            vsb[:].rearrange("p (h c) -> p h c", c=66)[:, :, 64:65], 1.0),
            reads=[], writes=[B("vones")])
        for t in range(NT):
            seq = t // SEQT
            first = (t % SEQT) == 0
            last = (t % SEQT) == SEQT - 1
            dma("sp", xbuf[:], x1_h[t * 128:(t + 1) * 128, :],
                reads=[B("x1_" + str(t))], writes=[bx], stream="xio")
            if os.environ.get("KV1"):
                vo("dve", lambda e: e.tensor_copy(xln[:], xbuf[:]), reads=[bx], writes=[bxln])
            else:
                vo("dve", lambda e, tt=t: e.tensor_scalar(xln[:], xbuf[:],
                    meanA[:, tt:tt + 1], rstdA[:, tt:tt + 1], ALU.subtract, ALU.mult),
                    reads=[bx, B("rstdA"), B("meanA_" + str(t))], writes=[bxln])
            vo("dve", lambda e: e.tensor_copy(xlnb[:], xln[:]), reads=[bxln], writes=[bxlnb])
            for kc in range(NKC):
                trans(xlnb[:, ts32(kc)], reads=[bxlnb] + ([bqf, bkf, bvs] if t > 0 else []))
                vo("dve", lambda e, kk=kc: e.tensor_copy(xlnT[:, ts32(kk)], psT[:]),
                    reads=[B("psT")], writes=[bxlnT] if kc == 0 else (),
                    append=() if kc == 0 else [bxlnT])
            if KA < 2:
                continue
            for h in range(H):
                par, j = h % 2, h // 2
                bank = ps["E"] if j < 4 else ps["F"]
                pb = pE if j < 4 else pF
                st = (par == 0 and j % 4 == 0)
                mm(bank[64 * par:64 * par + 64, 128 * (j % 4):128 * (j % 4) + 128],
                   wqs[:, h * DH:(h + 1) * DH], xlnT[:, ts32(h // 2)],
                   start=True, stop=True, reads=[B("wqs"), bxlnT],
                   writes=[pb] if st else (), append=() if st else [pb],
                   tp=(0, 64 * par))
            KPHI = int(os.environ.get("KPHI", "4"))
            def phi(dst_ap, dst_b, bankL, bankR, pbL, pbR):
                vo("dve", lambda e, s=bankL: e.tensor_single_scalar(
                    xln[:, 0:512], s[:], 0.0, ALU.min), reads=[pbL], writes=[bph1])
                vo("dve", lambda e, s=bankR: e.tensor_single_scalar(
                    xln[:, 512:1024], s[:], 0.0, ALU.min), reads=[pbR], append=[bph1])
                if KPHI >= 2:
                    if os.environ.get("KV4"):
                        act(sgbuf[:], xln[:], AF.Relu, reads=[bph1], writes=[bph2])
                    elif os.environ.get("KV5"):
                        act(sgbuf[:, 0:512], bankL[:], AF.Relu, reads=[pbL, bph1], writes=[bph2])
                        act(sgbuf[:, 512:1024], bankR[:], AF.Relu, reads=[pbR], append=[bph2])
                    else:
                        act(sgbuf[:, 0:512], bankL[:], AF.Relu, reads=[pbL], writes=[bph2])
                        act(sgbuf[:, 512:1024], bankR[:], AF.Relu, reads=[pbR], append=[bph2])
                if KPHI >= 3:
                    act(xln[:], xln[:], AF.Exp, reads=[bph1], writes=[bph1])
                if KPHI >= 4:
                    vo("dve", lambda e, d=dst_ap: e.tensor_tensor(d, xln[:], sgbuf[:], ALU.add),
                        reads=[bph1, bph2], writes=[dst_b])
                else:
                    vo("dve", lambda e, d=dst_ap: e.tensor_copy(d, xln[:]),
                        reads=[bph1], writes=[dst_b])
            if KA < 3:
                continue
            phi(qfTb[:], bqf, ps["E"], ps["F"], pE, pF)
            KQF = int(os.environ.get("KQF", "2"))
            if KQF == 1:
                dma("sp", qfT_h[t], qfTb[:], reads=[bqf], writes=[B("qfT_" + str(t))], stream="qfT")
            elif KQF == 2:
                dma("act", qfT_h[t], qfTb[:], reads=[bqf], writes=[B("qfT_" + str(t))], stream="qfT")
            if KA < 4:
                continue
            for h in range(H):
                bank = ps["A"] if h < 8 else ps["B"]
                pb = pA if h < 8 else pB
                st = (h % 8 == 0)
                mm(bank[:, 64 * (h % 8):64 * (h % 8) + 64], xlnT[:, ts32(h // 2)],
                   wks[:, h * DH:(h + 1) * DH], start=True, stop=True,
                   reads=[B("wks"), bxlnT], writes=[pb] if st else (),
                   append=() if st else [pb])
            phi(kf[:], bkf, ps["A"], ps["B"], pA, pB)
            for h in range(H):
                bank = ps["A"] if h < 8 else ps["B"]
                pb = pA if h < 8 else pB
                st = (h % 8 == 0)
                mm(bank[:, 64 * (h % 8):64 * (h % 8) + 64], xlnT[:, ts32(h // 2)],
                   wvs[:, h * DH:(h + 1) * DH], start=True, stop=True,
                   reads=[B("wvs"), bxlnT], writes=[pb] if st else (),
                   append=() if st else [pb])
            for bi, (bank, pb) in enumerate(((ps["A"], pA), (ps["B"], pB))):
                vo("dve", lambda e, b=bank, off=bi * 8: e.tensor_copy(
                    vsb[:].rearrange("p (h c) -> p h c", c=66)[:, off:off + 8, 0:64],
                    b[:].rearrange("p (h c) -> p h c", c=64)),
                    reads=[pb], writes=[bvs] if bi == 0 else (),
                    append=() if bi == 0 else [bvs])
            if KA < 6:
                continue
            pD2 = B("psD")
            for h in range(H):
                if h < 7:
                    bank, pb, slot = ps["G"], pG, h
                elif h < 14:
                    bank, pb, slot = ps["C"], pC, h - 7
                else:
                    bank, pb, slot = ps["D"], pD2, h - 14
                st = (first and slot == 0)
                mm(bank[0:64, 65 * slot:65 * slot + 65],
                   kf[:, h * DH:(h + 1) * DH], vsb[:, h * 66:h * 66 + 65],
                   start=first, stop=last,
                   reads=[bkf, bvs, B("vones")],
                   writes=[pb] if st else (), append=() if st else [pb])
            if last:
                for bi, (bank, pb, nsl) in enumerate(((ps["G"], pG, 7), (ps["C"], pC, 7),
                                                      (ps["D"], pD2, 2))):
                    h0 = (0, 7, 14)[bi]
                    vo("dve", lambda e, b=bank, s=seq, hh0=h0, nn=nsl: e.tensor_copy(
                        kvsb[0:64, s * 1056 + 66 * hh0: s * 1056 + 66 * (hh0 + nn)].rearrange(
                            "p (s2 c) -> p s2 c", c=66)[:, :, 0:65],
                        b[0:64, 0:65 * nn].rearrange("p (s2 c) -> p s2 c", c=65)),
                        reads=[pb], writes=[B("kv_%d_%d" % (seq, bi))])

    def attn_b():
        import os
        KB = int(os.environ.get("KB", "9"))
        bx, bqf = B("xbuf"), B("qfTb")
        bctx, bctxT = B("qfTb"), B("kfb")
        bden, brden = B("dens"), B("rdens")
        pE, pF, pA = B("psE"), B("psF"), B("psA")
        pC, pD = B("psC"), B("psD")
        bwos = B("wos")
        for t in range(NT):
            seq = t // SEQT
            kvr = [B("kv_%d_%d" % (seq, bi)) for bi in range(3)]
            dma("sp", xbuf[:], x1_h[t * 128:(t + 1) * 128, :],
                reads=[B("x1_" + str(t))], writes=[bx], stream="xio")
            dma("act", qfTb2.ap().rearrange("d (b par c) -> d b par c", par=2, c=128),
                qfT_h[t].rearrange("(par d) (b c) -> d b par c", par=2, c=128),
                reads=[B("qfT_" + str(t))], writes=[bqf], stream="qfT")
            if KB < 2:
                dma("sp", x2_h[t * 128:(t + 1) * 128, :], xbuf[:], reads=[bx],
                    writes=[B("x2_" + str(t))], stream="xio")
                continue
            for h in range(H):
                par, j = h % 2, h // 2
                if h < 7:
                    bank, pb, slot = ps["E"], pE, h
                elif h < 14:
                    bank, pb, slot = ps["F"], pF, h - 7
                else:
                    bank, pb, slot = ps["A"], pA, h - 14
                mm(bank[:, slot * 65:slot * 65 + 65],
                   qfTb2[0:64, 128 * h:128 * h + 128],
                   kvsb[0:64, seq * 1056 + 66 * h: seq * 1056 + 66 * h + 65],
                   start=True, stop=True, reads=[bqf] + kvr,
                   writes=[pb] if slot == 0 else (), append=() if slot == 0 else [pb])
            if KB < 3:
                vo("dve", lambda e: e.tensor_copy(xbuf[:, 0:455], ps["E"][:, 0:455]),
                    reads=[pE], writes=[bx])
                vo("dve", lambda e: e.tensor_copy(xbuf[:, 455:910], ps["F"][:, 0:455]),
                    reads=[pF], append=[bx])
                vo("dve", lambda e: e.tensor_copy(xbuf[:, 910:1024], ps["A"][:, 0:114]),
                    reads=[pA], append=[bx])
                dma("sp", x2_h[t * 128:(t + 1) * 128, :], xbuf[:], reads=[bx],
                    writes=[B("x2_" + str(t))], stream="xio")
                continue
            vo("dve", lambda e: e.tensor_copy(dens[:, 0:7],
                ps["E"][:, 0:455].rearrange("p (s c) -> p s c", c=65)[:, :, 64]),
                reads=[pE], writes=[bden])
            vo("dve", lambda e: e.tensor_copy(dens[:, 7:14],
                ps["F"][:, 0:455].rearrange("p (s c) -> p s c", c=65)[:, :, 64]),
                reads=[pF], append=[bden])
            vo("dve", lambda e: e.tensor_copy(dens[:, 14:16],
                ps["A"][:, 0:130].rearrange("p (s c) -> p s c", c=65)[:, :, 64]),
                reads=[pA], append=[bden])
            vo("dve", lambda e: e.tensor_single_scalar(dens[:], dens[:], EPS_ATT, ALU.add),
                reads=[bden], writes=[bden])
            vo("dve", lambda e: e.reciprocal(rden[:], dens[:]), reads=[bden], writes=[brden, bden])
            if KB < 4:
                dma("sp", x2_h[t * 128:(t + 1) * 128, :], xbuf[:], reads=[bx],
                    writes=[B("x2_" + str(t))], stream="xio")
                continue
            for (bank, pb, h0, nh) in ((ps["E"], pE, 0, 7), (ps["F"], pF, 7, 7),
                                       (ps["A"], pA, 14, 2)):
                rb = rden[:, h0:h0 + nh].unsqueeze(2).broadcast_to([128, nh, 64])
                vo("dve", lambda e, b=bank, hh=h0, nn=nh, r=rb: e.tensor_tensor(
                    ctx[:, hh * 64:(hh + nn) * 64].rearrange("p (s c) -> p s c", c=64),
                    b[:, 0:nn * 65].rearrange("p (s c) -> p s c", c=65)[:, :, 0:64],
                    r, ALU.mult),
                    reads=[pb, brden], writes=[bctx] if h0 == 0 else (),
                    append=() if h0 == 0 else [bctx])
            if KB < 5:
                dma("sp", x2_h[t * 128:(t + 1) * 128, :], xbuf[:], reads=[bx],
                    writes=[B("x2_" + str(t))], stream="xio")
                continue
            for kc in range(NKC):
                trans(ctx[:, ts32(kc)], reads=[bctx])
                vo("dve", lambda e, kk=kc: e.tensor_copy(ctxT[:, ts32(kk)], psT[:]),
                    reads=[B("psT")], writes=[bctxT] if kc == 0 else (),
                    append=() if kc == 0 else [bctxT])
            for kc in range(NKC):
                mm(ps["C"][:], ctxT[:, ts32(kc)], wos[:, kc * D: kc * D + 512],
                   start=(kc == 0), stop=(kc == NKC - 1), reads=[bctxT, bwos],
                   writes=[pC] if kc == 0 else (), append=() if kc == 0 else [pC])
            for kc in range(NKC):
                mm(ps["D"][:], ctxT[:, ts32(kc)], wos[:, kc * D + 512: kc * D + 1024],
                   start=(kc == 0), stop=(kc == NKC - 1), reads=[bctxT, bwos],
                   writes=[pD] if kc == 0 else (), append=() if kc == 0 else [pD])
            vo("dve", lambda e: e.tensor_tensor(xbuf[:, 0:512], xbuf[:, 0:512],
                ps["C"][:], ALU.add), reads=[pC], writes=[bx], append=())
            vo("dve", lambda e: e.tensor_tensor(xbuf[:, 512:1024], xbuf[:, 512:1024],
                ps["D"][:], ALU.add), reads=[pD], append=[bx])
            dma("sp", x2_h[t * 128:(t + 1) * 128, :], xbuf[:], reads=[bx],
                writes=[B("x2_" + str(t))], stream="xio")

    import os
    NPH = int(os.environ.get("KPHASES", "99"))
    init_consts()
    dma("sp", ident[:], id_h, reads=[], writes=[B("ident")], stream="ident")
    if os.environ.get("KV2"):
        vo("pool", lambda e: e.memset(wqs[:], 0.01), reads=[], writes=[B("wqs")])
        vo("pool", lambda e: e.memset(wks[:], 0.01), reads=[], writes=[B("wks")])
        vo("pool", lambda e: e.memset(wvs[:], 0.01), reads=[], writes=[B("wvs")])
    else:
        dma("pool", wqs[:], wqp_h, reads=[], writes=[B("wqs")], stream="wq")
        dma("pool", wks[:], wkp_h, reads=[], writes=[B("wks")], stream="wk")
        dma("pool", wvs[:], wvp_h, reads=[], writes=[B("wvs")], stream="wv")

    def _setphase(nm):
        P.phase = nm

    phases = [
        lambda: (_setphase("p1pre"), ln_stats(x0_h, None)),
        lambda: (_setphase("p0"), weight_quant(wg_h, NKI, D, wgT, "wgT")),
        lambda: weight_quant(wu_h, NKI, D, wuT, "wuT"),
        lambda: weight_quant(wd_h, NKC, I, wdT, "wdT"),
        lambda: (_setphase("p1"), ffn_phase(x0_h, None, x1_h, "x1")),
        lambda: (_setphase("p2pre"), ln_stats(x1_h, "x1")),
        lambda: (_setphase("p2a"), attn_a()),
        lambda: (vo("pool", lambda e: e.nop(), reads=[],
                    writes=[B("hq"), B("hqT"), B("xq2"), B("xqT2"), B("wos")]),
                 dma("pool", wos.rearrange("p (c o) -> p c o", o=D),
                     wo_h.rearrange("(c p) o -> p c o", p=128),
                     reads=[], writes=[B("wos")], stream="wos"),
                 _setphase("p2b"), attn_b()),
        lambda: (_setphase("p3pre"), ln_stats(x2_h, "x2")),
        lambda: (vo("pool", lambda e: e.nop(), reads=[],
                    writes=[B("hq"), B("hqT"), B("xq2"), B("xqT2"), B("wos")]),
                 _setphase("p3"), ffn_phase(x2_h, "x2", out_h, "out")),
    ]
    PSET = os.environ.get("KPSET", "")
    if PSET:
        sel = set(int(x) for x in PSET.split(","))
        for i, ph in enumerate(phases):
            if i in sel:
                ph()
    else:
        for i, ph in enumerate(phases):
            if i < NPH:
                ph()

    final_counts = dict(P.counts)

    sems = {k: nc.alloc_semaphore("sem_" + k) for k in P.sem_keys}
    eng_map = {"pe": "tensor", "act": "scalar", "dve": "vector",
               "pool": "gpsimd", "sp": "sync"}

    with nc.Block() as block:
        def emit_for(eng_name):
            def body(eng):
                cur = None
                import contextlib
                ctxm = [contextlib.nullcontext()]
                for (waits, fn, skey, amt, phn) in P.ops[eng_name]:
                    if phn != cur:
                        ctxm[0].__exit__(None, None, None) if cur is not None else None
                        ctxm[0] = nc.named_scope(phn)
                        ctxm[0].__enter__()
                        cur = phn
                    for (wk, wv) in waits:
                        eng.wait_ge(sems[wk], wv)
                    fn(eng).then_inc(sems[skey], amt)
                if cur is not None:
                    ctxm[0].__exit__(None, None, None)
                if eng_name == "sp":
                    for k in P.sem_keys:
                        if final_counts[k] > 0:
                            eng.wait_ge(sems[k], final_counts[k])
            return body

        for eng_name, attr in eng_map.items():
            getattr(block, attr)(emit_for(eng_name))

    return nc


_CACHE = {}


def kernel(L0_attn_query, L0_attn_Wq, L0_attn_Wk, L0_attn_Wv, L0_attn_Wo,
           L0_ffn_w_gate, L0_ffn_w_up, L0_ffn_w_down):
    x = np.ascontiguousarray(np.asarray(L0_attn_query, np.float32)).reshape(16, 2048, 1024)
    shards = [np.ascontiguousarray(x[2 * c:2 * c + 2].reshape(TPC, D)) for c in range(NCORES)]

    def pad_heads(w):
        w = np.asarray(w, np.float32)
        p = np.zeros((128, H, DH), np.float32)
        for h in range(H):
            p[64 * (h % 2):64 * (h % 2) + 64, h, :] = w[h]
        return np.ascontiguousarray(p.reshape(128, H * DH))

    common = {
        "wg": np.ascontiguousarray(np.asarray(L0_ffn_w_gate, np.float32)),
        "wu": np.ascontiguousarray(np.asarray(L0_ffn_w_up, np.float32)),
        "wd": np.ascontiguousarray(np.asarray(L0_ffn_w_down, np.float32)),
        "wqp": pad_heads(L0_attn_Wq),
        "wkp": pad_heads(L0_attn_Wk),
        "wvp": pad_heads(L0_attn_Wv),
        "wo": np.ascontiguousarray(np.asarray(L0_attn_Wo, np.float32)),
        "ident": np.eye(128, dtype=ml_dtypes.bfloat16),
    }
    if "nc" not in _CACHE:
        _CACHE["nc"] = build_nc()
    nc = _CACHE["nc"]
    in_maps = [{"x0": shards[c], **common} for c in range(NCORES)]
    res = run_bass_kernel_spmd(nc, in_maps, core_ids=list(range(NCORES)))
    out = np.empty((16, 2048, 1024), np.float32)
    for c in range(NCORES):
        out[2 * c:2 * c + 2] = np.asarray(res.results[c]["xout"], np.float32).reshape(2, 2048, 1024)
    return out
